# version 07
# baseline (speedup 1.0000x reference)
import math
from contextlib import ExitStack, contextmanager
import numpy as np
import ml_dtypes
import concourse.bass as bass
import concourse.mybir as mybir
from concourse.bass_utils import run_bass_kernel_spmd

F32 = mybir.dt.float32
BF16 = mybir.dt.bfloat16
AF = mybir.ActivationFunctionType
OP = mybir.AluOpType
AX = mybir.AxisListType
BF = ml_dtypes.bfloat16

D = 2048
DI = 4096
DH = 8192
NH = 32
HD = 128
NST = 64
CHK = 256
CONVD = DI + 2 * NST
DIP = 2 * DI + 2 * NST + NH
EPS = 1e-5
GRP = 512
NCORES = 8


def cdiv(a, b):
    return (a + b - 1) // b


class Sched:
    STREAMS = ("pe", "dve", "act", "pool", "sp")

    def __init__(self):
        self.ops = []

    def op(self, stream, fn, r=(), w=(), tag=""):
        self.ops.append(dict(s=stream, fn=fn, r=tuple(r), w=tuple(w),
                             dma=False, tag=tag, id=len(self.ops)))

    def dma(self, stream, fn, r=(), w=(), tag=""):
        self.ops.append(dict(s=stream, fn=fn, r=tuple(r), w=tuple(w),
                             dma=True, tag=tag, id=len(self.ops)))

    def barrier(self):
        n = len(self.ops)
        toks = [("__bar__", n, s) for s in self.STREAMS]
        for s, t in zip(self.STREAMS, toks):
            self.op(s, None, w=[t], tag="barw")
        for s in self.STREAMS:
            self.op(s, None, r=toks, tag="barr")

    NSLOT = 8

    def _analyze(self):
        slot_ctr = {}
        for o in self.ops:
            if o["dma"]:
                q = o["s"]
                n = slot_ctr.get(q, 0)
                slot_ctr[q] = n + 1
                o["sk"] = ("d", q, n % self.NSLOT)
                o["slot_idx"] = n // self.NSLOT + 1
            else:
                o["sk"] = ("c", o["s"])
        last_w = {}
        readers = {}
        deps = [set() for _ in self.ops]
        last_by_sem = {}
        for o in self.ops:
            i = o["id"]
            if o["fn"] is None and not o["dma"]:
                for k, j in last_by_sem.items():
                    if k[1] == o["s"]:
                        deps[i].add(j)
            if o["fn"] is not None:
                last_by_sem[o["sk"]] = i
            for b in o["r"]:
                if b in last_w:
                    deps[i].add(last_w[b])
            for b in o["w"]:
                if b in last_w:
                    deps[i].add(last_w[b])
                for rd in readers.get(b, ()):
                    deps[i].add(rd)
            for b in o["r"]:
                readers.setdefault(b, []).append(i)
            for b in o["w"]:
                last_w[b] = i
                readers[b] = []
        need_inc = set()
        for dd in deps:
            need_inc.update(dd)
        for o in self.ops:
            if o["dma"]:
                need_inc.add(o["id"])
        seq = {}
        for o in self.ops:
            if o["dma"]:
                o["inc_val"] = o["slot_idx"]
            elif o["id"] in need_inc:
                k = o["sk"]
                seq[k] = seq.get(k, 0) + 1
                o["inc_val"] = seq[k]
            else:
                o["inc_val"] = None
        seen = {}
        for o in self.ops:
            need = {}
            for j in deps[o["id"]]:
                k = self.ops[j]["sk"]
                need[k] = max(need.get(k, 0), self.ops[j]["inc_val"])
            waits = []
            for k, v in need.items():
                key = (o["sk"], k)
                if seen.get(key, 0) < v:
                    seen[key] = v
                    waits.append((k, v))
            o["waits"] = waits

    def emit(self, nc):
        self._analyze()
        self._sems = {}
        for st in self.STREAMS:
            cm = nc.semaphore(f"sem_c_{st}")
            self._sems[("c", st)] = cm.__enter__()
            for sl in range(self.NSLOT):
                cm = nc.semaphore(f"sem_d_{st}{sl}")
                self._sems[("d", st, sl)] = cm.__enter__()
        blk = nc.Block()
        block = blk.__enter__()
        sections = dict(pe=block.tensor, dve=block.vector, act=block.scalar,
                        pool=block.gpsimd, sp=block.sync)

        def mk(sname):
            def body(eng):
                for o in self.ops:
                    if o["s"] != sname:
                        continue
                    for (k, v) in o["waits"]:
                        eng.wait_ge(self._sems[k], v * (16 if k[0] == "d" else 1))
                    if o["dma"] and o["slot_idx"] > 1:
                        eng.wait_ge(self._sems[o["sk"]], (o["slot_idx"] - 1) * 16)
                    ins = o["fn"](eng) if o["fn"] is not None else None
                    if o["inc_val"] is not None:
                        tgt = self._sems[o["sk"]]
                        if ins is not None:
                            ins.then_inc(tgt, 16 if o["dma"] else 1)
                        else:
                            eng.nop().then_inc(tgt, 1)
            return body
        for sname in self.STREAMS:
            sections[sname](mk(sname))
        blk.__exit__(None, None, None)


class Ctx:

    def __init__(self, nc, ntok):
        self.nc = nc
        self.ntok = ntok
        self.sch = Sched()
        self._uid = 0
        self.onesrow = nc.alloc_sbuf_tensor("onesrow", [1, GRP], F32)
        self.sumcol = nc.alloc_sbuf_tensor("sumcol", [128, 64], F32)
        self.sumsqcol = nc.alloc_sbuf_tensor("sumsqcol", [128, 64], F32)
        self.ps = [nc.alloc_psum_tensor(f"psb{i}", [128, 512], F32) for i in range(8)]
        self.sch.op("dve", lambda e: e.memset(self.onesrow[:], 1.0),
                    w=[("ones",)], tag="ones")

    def name(self, p):
        self._uid += 1
        return f"{p}_{self._uid}"

    def sb(self, p, shape, dtype):
        if getattr(self, "stack", None) is not None:
            return self.stack.enter_context(
                self.nc.sbuf_tensor(self.name(p), shape, dtype))
        return self.nc.alloc_sbuf_tensor(self.name(p), shape, dtype)

    @contextmanager
    def phase(self):
        prev = getattr(self, "stack", None)
        self.stack = ExitStack()
        try:
            yield
        finally:
            self.stack.close()
            self.stack = prev
            self.sch.barrier()


def dram_tile_keys(tname, row0, rows):
    t0 = row0 // 128
    t1 = (row0 + rows - 1) // 128
    return [("dram", tname, t) for t in range(t0, t1 + 1)]


def build_ffn_phase(cx, w0s, w1s, b0_dram, b1_dram, src_dram, src_name,
                    out_dram, out_name, ntok_all, tag):
    nc, sch = cx.nc, cx.sch
    ntile = cdiv(ntok_all, 128)
    ngrp = cdiv(ntok_all, GRP)
    T = tag

    b0row_sb = cx.sb(T + "b0r", [1, DH], F32)
    b1row_sb = cx.sb(T + "b1r", [1, D], F32)
    sch.dma("sp", lambda e: e.dma_start(out=b0row_sb[:], in_=b0_dram), w=[("b0", T)],
            tag=T + "ldb0")
    sch.dma("sp", lambda e: e.dma_start(out=b1row_sb[:], in_=b1_dram), w=[("b1", T)],
            tag=T + "ldb1")
    mu = cx.sb(T + "mu", [128, ntile], F32)
    var = cx.sb(T + "var", [128, ntile], F32)
    rstd = cx.sb(T + "rstd", [128, ntile], F32)
    xf = cx.sb(T + "xf", [128, 16 * GRP], BF16)
    Hb = cx.sb(T + "H", [128, 64 * GRP], BF16)
    xtok = [cx.sb(T + "xt", [128, D], F32) for _ in range(3)]
    statx = xtok
    xhat = [cx.sb(T + "xh", [128, D], BF16) for _ in range(3)]
    w0b = [cx.sb(T + "w0", [128, 16 * 128], BF16) for _ in range(2)]
    w1b = [cx.sb(T + "w1", [128, 2048], BF16) for _ in range(2)]
    ot = [cx.sb(T + "ot", [128, D], F32) for _ in range(4)]
    ps = cx.ps

    sch.op("dve", lambda e: e.memset(var[:], 1.0), w=[(T, "var", "init")], tag=f"{T}vini")
    for i in range(3):
        sch.op("dve", (lambda i=i: lambda e: e.memset(xhat[i][:], 0.0))(),
               w=[(T, "xhb", i)], tag=f"{T}xhini{i}")
    for t in range(ntile):
        rows = min(128, ntok_all - t * 128)
        xt = statx[t % 3]
        sch.dma("sp", (lambda xt=xt, t=t, rows=rows: lambda e:
                       e.dma_start(out=xt[0:rows, :], in_=src_dram[t * 128:t * 128 + rows, :]))(),
                r=dram_tile_keys(src_name, t * 128, rows), w=[(T, "xtk", t % 3)],
                tag=f"{T}lds{t}")
        sch.op("dve", (lambda xt=xt, t=t, rows=rows: lambda e:
                       e.reduce_sum(cx.sumcol[0:rows, t % 64:t % 64 + 1], xt[0:rows, :], axis=AX.X))(),
               r=[(T, "xtk", t % 3)], w=[(T, "sum", t)], tag=f"{T}sum{t}")
        sch.op("act", (lambda xt=xt, t=t, rows=rows: lambda e:
                       e.activation(xt[0:rows, :], xt[0:rows, :], AF.Square,
                                    accum_out=cx.sumsqcol[0:rows, t % 64:t % 64 + 1]))(),
               r=[(T, "xtk", t % 3), (T, "sum", t)], w=[(T, "xtk", t % 3), (T, "ssq", t)],
               tag=f"{T}ssq{t}")
        sch.op("dve", (lambda t=t, rows=rows: lambda e:
                       e.tensor_scalar(mu[0:rows, t:t + 1], cx.sumcol[0:rows, t % 64:t % 64 + 1],
                                       1.0 / D, None, OP.mult))(),
               r=[(T, "sum", t)], w=[(T, "mu", t)], tag=f"{T}mu{t}")
        sch.op("dve", (lambda t=t, rows=rows: lambda e:
                       e.tensor_tensor(var[0:rows, t:t + 1], mu[0:rows, t:t + 1],
                                       mu[0:rows, t:t + 1], OP.mult))(),
               r=[(T, "mu", t)], w=[(T, "m2", t)], tag=f"{T}m2_{t}")
        sch.op("dve", (lambda t=t, rows=rows: lambda e:
                       e.scalar_tensor_tensor(var[0:rows, t:t + 1],
                                              cx.sumsqcol[0:rows, t % 64:t % 64 + 1], 1.0 / D,
                                              var[0:rows, t:t + 1], OP.mult, OP.subtract))(),
               r=[(T, "ssq", t), (T, "m2", t)], w=[(T, "varp", t)], tag=f"{T}var{t}")
        sch.op("dve", (lambda t=t, rows=rows: lambda e:
                       e.tensor_scalar(var[0:rows, t:t + 1], var[0:rows, t:t + 1],
                                       EPS, None, OP.add))(),
               r=[(T, "varp", t), (T, "var", "init")], w=[(T, "var", t)], tag=f"{T}vep{t}")
    sch.op("act", lambda e: e.activation(rstd[:, 0:ntile], var[:, 0:ntile], AF.Sqrt),
           r=[(T, "var", t) for t in range(ntile)], w=[(T, "sd")], tag=f"{T}sqrt")
    sch.op("dve", lambda e: e.reciprocal(rstd[:, 0:ntile], rstd[:, 0:ntile]),
           r=[(T, "sd")], w=[(T, "rstd")], tag=f"{T}rinv")

    for g in range(ngrp):
        t0g = g * GRP
        gtok = min(GRP, ntok_all - t0g)
        gtile = cdiv(gtok, 128)
        for tt in range(gtile):
            t = (t0g // 128) + tt
            rows = min(128, gtok - tt * 128)
            xt = xtok[t % 3]
            xh = xhat[t % 3]
            sch.dma("sp", (lambda xt=xt, t=t, rows=rows: lambda e:
                           e.dma_start(out=xt[0:rows, :],
                                       in_=src_dram[t * 128:t * 128 + rows, :]))(),
                    r=dram_tile_keys(src_name, t * 128, rows), w=[(T, "xtk", t % 3)],
                    tag=f"{T}ldx{t}")
            sch.op("dve", (lambda xt=xt, xh=xh, t=t, rows=rows: lambda e:
                           e.tensor_scalar(xh[0:rows, :], xt[0:rows, :], mu[0:rows, t:t + 1],
                                           rstd[0:rows, t:t + 1], OP.subtract, OP.mult))(),
                   r=[(T, "xtk", t % 3), (T, "mu", t), (T, "rstd")],
                   w=[(T, "xhb", t % 3)], tag=f"{T}nrm{t}")
            trows = rows if rows % 16 == 0 else min(128, (rows // 16 + 1) * 16)
            sch.dma("act", (lambda xh=xh, tt=tt, trows=trows: lambda e:
                            e.dma_start_transpose(
                                out=xf[:].rearrange("p (k t) -> p k t", k=16)
                                    [:, :, tt * 128:tt * 128 + trows],
                                in_=xh[0:trows, :].rearrange("t (k p) -> t k p", k=16)))(),
                    r=[(T, "xhb", t % 3)], w=[(T, "xf", g, tt)], tag=f"{T}tr{t}")
        xf_keys = [(T, "xf", g, tt) for tt in range(gtile)]

        for m in range(64):
            wb = w0b[m % 2]
            sch.dma("sp", (lambda m=m, wb=wb: lambda e:
                           e.dma_start(out=wb[:],
                                       in_=w0s[m].rearrange("p k j -> p (k j)")))(),
                    r=[("dram", "w0bs", m)], w=[(T, "w0b", m % 2)],
                    tag=f"{T}lw0_{g}_{m}")
            bank = m % 8

            def mm1(e, m=m, wb=wb, bank=bank, gtok=gtok):
                i = e.matmul(ps[bank][:, 0:gtok], b0row_sb[0:1, m * 128:(m + 1) * 128],
                             cx.onesrow[0:1, 0:gtok], start=True, stop=False)
                for kc in range(16):
                    i = e.matmul(ps[bank][:, 0:gtok], wb[:, kc * 128:(kc + 1) * 128],
                                 xf[:, kc * GRP:kc * GRP + gtok],
                                 start=False, stop=(kc == 15))
                return i
            sch.op("pe", mm1, r=[(T, "w0b", m % 2), ("b0", T), ("ones",)] + xf_keys,
                   w=[("psum", bank)], tag=f"{T}mm1_{g}_{m}")
            sch.op("act", (lambda m=m, bank=bank, gtok=gtok: lambda e:
                           e.activation(Hb[:, m * GRP:m * GRP + gtok],
                                        ps[bank][:, 0:gtok], AF.Gelu))(),
                   r=[("psum", bank)], w=[(T, "H", g, m)],
                   tag=f"{T}gel_{g}_{m}")
        H_keys = [(T, "H", g, m) for m in range(64)]

        for rnd in range(4):
            f0 = rnd * 512
            banks = [4 * (rnd % 2) + tt for tt in range(gtile)]
            for kc2 in range(64):
                wb = w1b[(kc2 // 4) % 2]
                if kc2 % 4 == 0:
                    sch.dma("act", (lambda kc2=kc2, f0=f0, wb=wb: lambda e:
                                    e.dma_start(out=wb[:].rearrange("p (q f) -> p q f", q=4),
                                                in_=w1s[kc2:kc2 + 4]
                                                .rearrange("q p f j -> p q (f j)")
                                                [:, :, f0:f0 + 512]))(),
                            r=[("dram", "w1bs", kc2 + qq) for qq in range(4)],
                            w=[(T, "w1b", (kc2 // 4) % 2)],
                            tag=f"{T}lw1_{g}_{rnd}_{kc2}")

                def mm2(e, kc2=kc2, f0=f0, wb=wb, gtile=gtile, gtok=gtok, banks=banks):
                    i = None
                    woff = (kc2 % 4) * 512
                    for tt in range(gtile):
                        rows = min(128, gtok - tt * 128)
                        bank = banks[tt]
                        if kc2 == 0:
                            i = e.matmul(ps[bank][0:rows, 0:512], cx.onesrow[0:1, 0:rows],
                                         b1row_sb[0:1, f0:f0 + 512], start=True, stop=False)
                        i = e.matmul(ps[bank][0:rows, 0:512],
                                     Hb[:, kc2 * GRP + tt * 128:kc2 * GRP + tt * 128 + rows],
                                     wb[:, woff:woff + 512], start=False, stop=(kc2 == 63))
                    return i
                wkeys = [("psum", b) for b in banks] if kc2 in (0, 63) else []
                sch.op("pe", mm2, r=[(T, "w1b", (kc2 // 4) % 2), ("b1", T), ("ones",)] + H_keys,
                       w=wkeys, tag=f"{T}mm2_{g}_{rnd}_{kc2}")
            for tt in range(gtile):
                rows = min(128, gtok - tt * 128)
                t = (t0g // 128) + tt
                bank = banks[tt]
                otb = ot[tt]
                if rnd == 0:
                    sch.dma("act", (lambda otb=otb, t=t, rows=rows: lambda e:
                                    e.dma_start(out=otb[0:rows, :],
                                                in_=src_dram[t * 128:t * 128 + rows, :]))(),
                            r=dram_tile_keys(src_name, t * 128, rows),
                            w=[(T, "otb", tt, "r")], tag=f"{T}ldr{t}")
                sch.op("dve", (lambda bank=bank, otb=otb, rows=rows, f0=f0: lambda e:
                               e.scalar_tensor_tensor(otb[0:rows, f0:f0 + 512],
                                                      ps[bank][0:rows, 0:512], 0.5,
                                                      otb[0:rows, f0:f0 + 512],
                                                      OP.mult, OP.add))(),
                       r=[("psum", bank), (T, "otb", tt, "r")],
                       w=[(T, "otb", tt, rnd)],
                       tag=f"{T}add_{g}_{rnd}_{tt}")
                if rnd == 3:
                    sch.dma("act", (lambda otb=otb, t=t, rows=rows: lambda e:
                                    e.dma_start(out=out_dram[t * 128:t * 128 + rows, :],
                                                in_=otb[0:rows, :]))(),
                            r=[(T, "otb", tt, rr) for rr in range(4)] + [(T, "otb", tt, "r")],
                            w=dram_tile_keys(out_name, t * 128, rows),
                            tag=f"{T}st{t}")


def prep_w0(w0):
    return np.ascontiguousarray(
        w0.reshape(64, 128, 16, 128).transpose(0, 3, 2, 1))


def prep_w1(w1):
    return np.ascontiguousarray(
        w1.reshape(16, 128, 64, 128).transpose(2, 3, 0, 1))


def ffn_reference(x, W0, b0, W1, b1):
    from scipy.special import erf
    ln = (x - x.mean(-1, keepdims=True)) / np.sqrt(x.var(-1, keepdims=True) + EPS)
    h = ln @ W0.T + b0
    Dh = (0.5 * h * (1.0 + erf(h / np.sqrt(2.0)))).astype(np.float32)
    return x + 0.5 * (Dh @ W1.T + b1)


NCH = 33


def build_p2a(cx, W, x1_dram, x1_name, outs, ntok, tag="a"):
    nc, sch = cx.nc, cx.sch
    T = tag
    ntok_all = ntok + 3
    ntile = cdiv(ntok_all, 128)
    nchunk = ntok // CHK
    ngrp = ntok // GRP
    ps = cx.ps

    mu = cx.sb(T + "mu", [128, ntile], F32)
    var = cx.sb(T + "var", [128, ntile], F32)
    rstd = cx.sb(T + "rstd", [128, ntile], F32)
    xtok = [cx.sb(T + "xt", [128, D], F32) for _ in range(2)]
    xhat = [cx.sb(T + "xh", [128, D], BF16) for _ in range(2)]
    xf = cx.sb(T + "xf", [128, 16 * GRP], BF16)
    xfh = cx.sb(T + "xfh", [128, 16 * 16], BF16)
    wxb = [cx.sb(T + "wx", [128, 16 * 128], BF16) for _ in range(2)]
    wzb = [cx.sb(T + "wz", [128, 4 * 512], BF16) for _ in range(2)]
    wdtb = cx.sb(T + "wdt", [128, 16 * 32], BF16)
    rawf = [cx.sb(T + "raw", [128, CHK], F32) for _ in range(2)]
    carry = cx.sb(T + "carry", [128, NCH * 4], F32)
    cvw = cx.sb(T + "cvw", [128, NCH * 4], F32)
    cvb = cx.sb(T + "cvb", [128, NCH], F32)
    cnv = [cx.sb(T + "cnv", [128, CHK], F32) for _ in range(2)]
    xsil = [cx.sb(T + "xsil", [128, NCH * CHK], BF16) for _ in range(2)]
    xtr = cx.sb(T + "xtr", [128, 8192], BF16)
    xst = cx.sb(T + "xst", [128, 8192], BF16)
    xdd = cx.sb(T + "xdd", [128, 8192], BF16)
    btb = cx.sb(T + "bt", [128, 128], BF16)
    zt = [cx.sb(T + "zt", [128, 512], BF16) for _ in range(2)]
    dtrow = cx.sb(T + "dtrow", [1, 64], F32)
    dts = cx.sb(T + "dts", [128, (2 * nchunk + 2) * 32], F32)
    atb = cx.sb(T + "atb", [128, 64], F32)
    acs = cx.sb(T + "acs", [128, nchunk * 64], F32)
    dsb = cx.sb(T + "ds", [128, nchunk * 64], F32)
    alast = cx.sb(T + "alast", [1, nchunk * 32], F32)
    alrow = cx.sb(T + "alrow", [1, 32], F32)
    arow = cx.sb(T + "arow", [1, 32], F32)
    abc = cx.sb(T + "abc", [128, 32], F32)
    albc = cx.sb(T + "albc", [128, 32], F32)
    dcol = cx.sb(T + "dcol", [64, 32], F32)
    rdec = cx.sb(T + "rdec", [1, 32], F32)
    Sst = [cx.sb(T + "S", [64, DI], F32) for _ in range(2)]
    onescolf = cx.sb(T + "onescol", [128, 1], F32)
    tri = cx.sb(T + "tri", [128, 128], F32)
    onesq = cx.sb(T + "onesq", [128, 128], F32)
    xbcbc = cx.sb(T + "xbcbc", [128, NCH], F32)
    zbr = [cx.sb(T + "zbr", [1, 512], F32) for _ in range(2)]
    hmask = cx.sb(T + "hmask", [128, 1], F32)

    for nm, dst, src in [("cvw", cvw, W["cwp"].rearrange("p m t -> p (m t)")),
                         ("cvb", cvb, W["cbp"]), ("tri", tri, W["tri"]),
                         ("onesq", onesq, W["onesq"]), ("xbcbc", xbcbc, W["xbcbc"]),
                         ("hmask", hmask, W["hmask"]),
                         ("arow0", arow, W["alog"])]:
        sch.dma("sp", (lambda dst=dst, src=src: lambda e: e.dma_start(out=dst[:], in_=src))(),
                w=[(T, nm)], tag=T + "ld" + nm)
    sch.dma("pool", lambda e: e.dma_start(out=wdtb[:], in_=W["wipdt"].rearrange("p k j -> p (k j)")),
            w=[(T, "wdt")], tag=T + "ldwdt")
    sch.dma("sp", lambda e: e.dma_start(out=dtrow[0:1, 0:32], in_=W["dtb"]),
            w=[(T, "dtb1")], tag=T + "ldtb1")
    sch.dma("sp", lambda e: e.dma_start(out=dtrow[0:1, 32:64], in_=W["dtbias"]),
            w=[(T, "dtb2")], tag=T + "ldtb2")
    sch.op("dve", lambda e: e.tensor_tensor(dtrow[0:1, 0:32], dtrow[0:1, 0:32],
                                            dtrow[0:1, 32:64], OP.add),
           r=[(T, "dtb1"), (T, "dtb2")], w=[(T, "dtbt")], tag=T + "dtbsum")
    sch.op("dve", lambda e: e.memset(onescolf[:], 1.0), w=[(T, "onec")], tag=T + "onec")
    sch.op("dve", lambda e: e.memset(Sst[0][:], 0.0), w=[(T, "S")], tag=T + "s0")
    sch.op("dve", lambda e: e.memset(rdec[:], 0.0), w=[(T, "rdecr")], tag=T + "rd0")
    sch.op("dve", lambda e: e.memset(var[:], 1.0), w=[(T, "var", "init")], tag=T + "vini")
    for i in range(2):
        sch.op("dve", (lambda i=i: lambda e: e.memset(xhat[i][:], 0.0))(),
               w=[(T, "xhb", i)], tag=T + f"xhini{i}")
    sch.op("act", lambda e: e.activation(arow[:], arow[:], AF.Exp),
           r=[(T, "arow0")], w=[(T, "arow1")], tag=T + "aexp")
    sch.op("dve", lambda e: e.tensor_scalar(arow[:], arow[:], -1.0, None, OP.mult),
           r=[(T, "arow1")], w=[(T, "arow")], tag=T + "aneg")
    sch.op("pe", lambda e: e.matmul(ps[5][0:128, 0:32], cx.onesrow[0:1, 0:128], arow[:],
                                    start=True, stop=True),
           r=[(T, "arow"), ("ones",)], w=[("psum", 5)], tag=T + "abcm")
    sch.op("dve", lambda e: e.tensor_copy(abc[:], ps[5][0:128, 0:32]),
           r=[("psum", 5)], w=[(T, "abc")], tag=T + "abcc")

    for t in range(ntile):
        rows = min(128, ntok_all - t * 128)
        xt = xtok[t % 2]
        sch.dma("sp", (lambda xt=xt, t=t, rows=rows: lambda e:
                       e.dma_start(out=xt[0:rows, :], in_=x1_dram[t * 128:t * 128 + rows, :]))(),
                r=dram_tile_keys(x1_name, t * 128, rows), w=[(T, "xtk", t % 2)], tag=f"{T}lds{t}")
        sch.op("dve", (lambda xt=xt, t=t, rows=rows: lambda e:
                       e.reduce_sum(cx.sumcol[0:rows, t % 64:t % 64 + 1], xt[0:rows, :], axis=AX.X))(),
               r=[(T, "xtk", t % 2)], w=[(T, "sum", t)], tag=f"{T}sum{t}")
        sch.op("act", (lambda xt=xt, t=t, rows=rows: lambda e:
                       e.activation(xt[0:rows, :], xt[0:rows, :], AF.Square,
                                    accum_out=cx.sumsqcol[0:rows, t % 64:t % 64 + 1]))(),
               r=[(T, "xtk", t % 2), (T, "sum", t)], w=[(T, "xtk", t % 2), (T, "ssq", t)],
               tag=f"{T}ssq{t}")
        sch.op("dve", (lambda t=t, rows=rows: lambda e:
                       e.tensor_scalar(mu[0:rows, t:t + 1], cx.sumcol[0:rows, t % 64:t % 64 + 1],
                                       1.0 / D, None, OP.mult))(),
               r=[(T, "sum", t)], w=[(T, "mu", t)], tag=f"{T}mu{t}")
        sch.op("dve", (lambda t=t, rows=rows: lambda e:
                       e.tensor_tensor(var[0:rows, t:t + 1], mu[0:rows, t:t + 1],
                                       mu[0:rows, t:t + 1], OP.mult))(),
               r=[(T, "mu", t)], w=[(T, "m2", t)], tag=f"{T}m2_{t}")
        sch.op("dve", (lambda t=t, rows=rows: lambda e:
                       e.scalar_tensor_tensor(var[0:rows, t:t + 1],
                                              cx.sumsqcol[0:rows, t % 64:t % 64 + 1], 1.0 / D,
                                              var[0:rows, t:t + 1], OP.mult, OP.subtract))(),
               r=[(T, "ssq", t), (T, "m2", t)], w=[(T, "varp", t)], tag=f"{T}var{t}")
        sch.op("dve", (lambda t=t, rows=rows: lambda e:
                       e.tensor_scalar(var[0:rows, t:t + 1], var[0:rows, t:t + 1],
                                       EPS, None, OP.add))(),
               r=[(T, "varp", t), (T, "var", "init")], w=[(T, "var", t)], tag=f"{T}vep{t}")
    sch.op("act", lambda e: e.activation(rstd[:, 0:ntile], var[:, 0:ntile], AF.Sqrt),
           r=[(T, "var", t) for t in range(ntile)], w=[(T, "sd")], tag=T + "sqrt")
    sch.op("dve", lambda e: e.reciprocal(rstd[:, 0:ntile], rstd[:, 0:ntile]),
           r=[(T, "sd")], w=[(T, "rstd")], tag=T + "rinv")

    def emit_ln_tile(t, xf_dst, dst_col, dkeys, grpw):
        rows = min(128, ntok_all - t * 128)
        xt = xtok[t % 2]
        xh = xhat[t % 2]
        sch.dma("sp", (lambda xt=xt, t=t, rows=rows: lambda e:
                       e.dma_start(out=xt[0:rows, :], in_=x1_dram[t * 128:t * 128 + rows, :]))(),
                r=dram_tile_keys(x1_name, t * 128, rows), w=[(T, "xtk", t % 2)], tag=f"{T}ldx{t}")
        sch.op("dve", (lambda xt=xt, xh=xh, t=t, rows=rows: lambda e:
                       e.tensor_scalar(xh[0:rows, :], xt[0:rows, :], mu[0:rows, t:t + 1],
                                       rstd[0:rows, t:t + 1], OP.subtract, OP.mult))(),
               r=[(T, "xtk", t % 2), (T, "mu", t), (T, "rstd")], w=[(T, "xhb", t % 2)],
               tag=f"{T}nrm{t}")
        trows = rows if rows % 16 == 0 else min(128, (rows // 16 + 1) * 16)
        sch.dma("act", (lambda xh=xh, trows=trows, xf_dst=xf_dst, dst_col=dst_col, grpw=grpw:
                        lambda e: e.dma_start_transpose(
                            out=xf_dst[:].rearrange("p (k t) -> p k t", k=16)
                                [:, :, dst_col:dst_col + trows],
                            in_=xh[0:trows, :].rearrange("t (k p) -> t k p", k=16)))(),
                r=[(T, "xhb", t % 2)], w=dkeys, tag=f"{T}tr{t}")

    halo_t = ntok // 128
    emit_ln_tile(halo_t, xfh, 0, [(T, "xfh")], 16)
    for m in range(NCH):
        wb = wxb[m % 2]
        sch.dma("sp", (lambda m=m, wb=wb: lambda e:
                       e.dma_start(out=wb[:], in_=W["wipx"][m].rearrange("p k j -> p (k j)")))(),
                r=[("dram", "wipxs", m)], w=[(T, "wxb", m % 2)], tag=f"{T}hlw{m}")

        def mmh(e, m=m, wb=wb):
            i = None
            for kc in range(16):
                i = e.matmul(ps[6][0:128, 0:3], wb[:, kc * 128:(kc + 1) * 128],
                             xfh[:, kc * 16:kc * 16 + 3], start=(kc == 0), stop=(kc == 15))
            return i
        sch.op("pe", mmh, r=[(T, "wxb", m % 2), (T, "xfh"), ("ones",)],
               w=[("psum", 6)], tag=f"{T}hmm{m}")
        sch.op("dve", (lambda m=m: lambda e:
                       e.tensor_scalar(carry[:, m * 4:m * 4 + 3], ps[6][0:128, 0:3],
                                       xbcbc[:, m:m + 1], hmask[:, 0:1],
                                       OP.add, OP.mult))(),
               r=[("psum", 6), (T, "hmask"), (T, "xbcbc")], w=[(T, "carry", m)],
               tag=f"{T}hcar{m}")

    for g in range(ngrp):
        for tt in range(GRP // 128):
            t = g * (GRP // 128) + tt
            emit_ln_tile(t, xf, tt * 128, [(T, "xf", g, tt)], GRP)
        xf_keys = [(T, "xf", g, tt) for tt in range(GRP // 128)]

        for zb in range(8):
            zr = zbr[zb % 2]
            sch.dma("sp", (lambda zb=zb, zr=zr: lambda e:
                           e.dma_start(out=zr[:], in_=W["zb"][zb:zb + 1, :]))(),
                    r=[("wipz",)], w=[(T, "zbr", zb % 2)], tag=f"{T}lzb{g}_{zb}")
            for kcq in range(4):
                wb = wzb[kcq % 2]
                sch.dma("act", (lambda zb=zb, kcq=kcq, wb=wb: lambda e:
                               e.dma_start(out=wb[:],
                                           in_=W["wipz"][zb, kcq]))(),
                        r=[("dram", "wipzs", zb, kcq)], w=[(T, "wzb", kcq % 2)],
                        tag=f"{T}lwz{g}_{zb}_{kcq}")

                def mmz(e, zb=zb, kcq=kcq, wb=wb, g=g, zr=zr):
                    i = None
                    for tt in range(4):
                        bank = 3 + tt
                        if kcq == 0:
                            i = e.matmul(ps[bank][0:128, 0:512], cx.onesrow[0:1, 0:128],
                                         zr[0:1, 0:512],
                                         start=True, stop=False)
                        for kk in range(4):
                            kc = kcq * 4 + kk
                            i = e.matmul(ps[bank][0:128, 0:512],
                                         xf[:, kc * GRP + tt * 128:kc * GRP + tt * 128 + 128],
                                         wb[:, kk * 512:(kk + 1) * 512],
                                         start=False, stop=(kc == 15))
                    return i
                wk = [("psum", 3 + tt) for tt in range(4)] if kcq in (0, 3) else []
                sch.op("pe", mmz, r=[(T, "wzb", kcq % 2), (T, "zbr", zb % 2), ("ones",)] + xf_keys,
                       w=wk, tag=f"{T}mmz{g}_{zb}_{kcq}")
            for tt in range(4):
                ztb = zt[tt % 2]
                sch.op("act", (lambda ztb=ztb, tt=tt: lambda e:
                               e.copy(ztb[:], ps[3 + tt][0:128, 0:512]))(),
                       r=[("psum", 3 + tt)], w=[(T, "ztb", tt % 2)], tag=f"{T}zev{g}_{zb}_{tt}")
                sch.dma("act", (lambda ztb=ztb, g=g, tt=tt, zb=zb: lambda e:
                                e.dma_start(out=outs["z"][g * GRP + tt * 128:
                                                          g * GRP + tt * 128 + 128,
                                                          zb * 512:(zb + 1) * 512],
                                            in_=ztb[:]))(),
                        r=[(T, "ztb", tt % 2)], w=[("dram", "z", g, tt, zb)],
                        tag=f"{T}sz{g}_{tt}_{zb}")

        for c in range(2 * g, 2 * g + 2):
            coff = (c % 2) * CHK
            xs_c = xsil[c % 2]
            for m in range(NCH):
                wb = wxb[m % 2]
                sch.dma("sp", (lambda m=m, wb=wb: lambda e:
                               e.dma_start(out=wb[:],
                                           in_=W["wipx"][m].rearrange("p k j -> p (k j)")))(),
                        r=[("dram", "wipxs", m)], w=[(T, "wxb", m % 2)], tag=f"{T}lwx{c}_{m}")
                bank = m % 2

                def mmx(e, m=m, wb=wb, bank=bank, coff=coff):
                    i = None
                    for kc in range(16):
                        i = e.matmul(ps[bank][0:128, 0:CHK], wb[:, kc * 128:(kc + 1) * 128],
                                     xf[:, kc * GRP + coff:kc * GRP + coff + CHK],
                                     start=(kc == 0), stop=(kc == 15))
                    return i
                sch.op("pe", mmx, r=[(T, "wxb", m % 2), ("ones",)] + xf_keys,
                       w=[("psum", bank)], tag=f"{T}mmx{c}_{m}")
                raw = rawf[m % 2]
                sch.op("dve", (lambda raw=raw, bank=bank, m=m: lambda e:
                               e.tensor_scalar(raw[:], ps[bank][0:128, 0:CHK],
                                               xbcbc[:, m:m + 1], None, OP.add))(),
                       r=[("psum", bank), (T, "xbcbc")], w=[(T, "raw", m % 2)],
                       tag=f"{T}rev{c}_{m}")
                cv = cnv[c % 2]
                sch.op("dve", (lambda raw=raw, cv=cv, m=m: lambda e:
                               e.tensor_scalar(cv[:], raw[:], cvw[:, m * 4 + 3:m * 4 + 4],
                                               None, OP.mult))(),
                       r=[(T, "raw", m % 2), (T, "cvw")], w=[(T, "cv", c % 2)],
                       tag=f"{T}cv3_{c}_{m}")
                for tap in range(3):
                    sh = 3 - tap
                    sch.op("dve", (lambda raw=raw, cv=cv, m=m, tap=tap, sh=sh: lambda e:
                                   e.scalar_tensor_tensor(cv[:, sh:CHK], raw[:, 0:CHK - sh],
                                                          cvw[:, m * 4 + tap:m * 4 + tap + 1],
                                                          cv[:, sh:CHK], OP.mult, OP.add))(),
                           r=[(T, "raw", m % 2), (T, "cvw"), (T, "cv", c % 2)],
                           w=[(T, "cv", c % 2)], tag=f"{T}cvt{c}_{m}_{tap}")
                    sch.op("dve", (lambda cv=cv, m=m, tap=tap, sh=sh: lambda e:
                                   e.scalar_tensor_tensor(cv[:, 0:sh],
                                                          carry[:, m * 4 + tap:m * 4 + 3],
                                                          cvw[:, m * 4 + tap:m * 4 + tap + 1],
                                                          cv[:, 0:sh], OP.mult, OP.add))(),
                           r=[(T, "carry", m), (T, "cvw"), (T, "cv", c % 2)],
                           w=[(T, "cv", c % 2)], tag=f"{T}cvc{c}_{m}_{tap}")
                sch.op("dve", (lambda raw=raw, m=m: lambda e:
                               e.tensor_copy(carry[:, m * 4:m * 4 + 3], raw[:, CHK - 3:CHK]))(),
                       r=[(T, "raw", m % 2), (T, "cv", c % 2)], w=[(T, "carry", m)],
                       tag=f"{T}cup{c}_{m}")
                sch.op("act", (lambda cv=cv, m=m, xs_c=xs_c: lambda e:
                               e.activation(xs_c[:, m * CHK:(m + 1) * CHK], cv[:],
                                            AF.Silu, bias=cvb[:, m:m + 1]))(),
                       r=[(T, "cv", c % 2), (T, "cvb")], w=[(T, "xsil", c % 2, m)],
                       tag=f"{T}sil{c}_{m}")
            sch.dma("act", (lambda c=c, xs_c=xs_c: lambda e:
                            e.dma_start(out=outs["bc"][:, c * CHK:(c + 1) * CHK],
                                        in_=xs_c[:, 32 * CHK:33 * CHK]))(),
                    r=[(T, "xsil", c % 2, 32)], w=[("dram", "bc", c)], tag=f"{T}sbc{c}")

            for sig in range(2):
                t2 = c * 2 + sig
                tok0 = coff + sig * 128

                def mmd(e, tok0=tok0):
                    i = e.matmul(ps[5][0:128, 64:96], cx.onesrow[0:1, 0:128],
                                 dtrow[0:1, 0:32], start=True, stop=False)
                    for kc in range(16):
                        i = e.matmul(ps[5][0:128, 64:96],
                                     xf[:, kc * GRP + tok0:kc * GRP + tok0 + 128],
                                     wdtb[:, kc * 32:(kc + 1) * 32], start=False,
                                     stop=(kc == 15))
                    return i
                sch.op("pe", mmd, r=[(T, "wdt"), (T, "dtbt"), ("ones",)] + xf_keys,
                       w=[("psum", 5)], tag=f"{T}mmd{t2}")
                dv = dts[:, t2 * 32:(t2 + 1) * 32]
                sch.op("act", (lambda dv=dv: lambda e:
                               e.activation(dv, ps[5][0:128, 64:96], AF.Abs))(),
                       r=[("psum", 5)], w=[(T, "dt1", t2)], tag=f"{T}dta{t2}")
                sch.op("act", (lambda dv=dv: lambda e:
                               e.activation(dv, dv, AF.Exp, scale=-1.0))(),
                       r=[(T, "dt1", t2)], w=[(T, "dt2", t2)], tag=f"{T}dte{t2}")
                sch.op("act", (lambda dv=dv: lambda e:
                               e.activation(dv, dv, AF.Ln, bias=1.0))(),
                       r=[(T, "dt2", t2)], w=[(T, "dt3", t2)], tag=f"{T}dtl{t2}")
                rl = dts[:, (2 * nchunk + sig) * 32:(2 * nchunk + sig + 1) * 32]
                sch.op("act", (lambda rl=rl: lambda e:
                               e.activation(rl, ps[5][0:128, 64:96], AF.Relu))(),
                       r=[("psum", 5), (T, "dt3", t2)], w=[(T, "dt4", t2), ("psum", 5)],
                       tag=f"{T}dtr{t2}")
                sch.op("dve", (lambda dv=dv, rl=rl: lambda e:
                               e.tensor_tensor(dv, dv, rl, OP.add))(),
                       r=[(T, "dt3", t2), (T, "dt4", t2)], w=[(T, "dts", t2)],
                       tag=f"{T}dtf{t2}")
                sch.op("dve", (lambda dv=dv, sig=sig: lambda e:
                               e.tensor_tensor(atb[:, sig * 32:(sig + 1) * 32], dv,
                                               abc[:], OP.mult))(),
                       r=[(T, "dts", t2), (T, "abc")], w=[(T, "at", c, sig)],
                       tag=f"{T}at{t2}")

            def mmacs0(e, c=c):
                return e.matmul(ps[5][0:128, 96:128], tri[:], atb[:, 0:32],
                                start=True, stop=True)
            sch.op("pe", mmacs0, r=[(T, "at", c, 0), (T, "tri")], w=[("psum", 5)],
                   tag=f"{T}acs0_{c}")
            sch.op("dve", (lambda c=c: lambda e:
                           e.tensor_copy(acs[:, c * 64:c * 64 + 32], ps[5][0:128, 96:128]))(),
                   r=[("psum", 5)], w=[(T, "acs", c, 0)], tag=f"{T}acsc0_{c}")

            def mmacs1(e, c=c):
                i = e.matmul(ps[5][0:128, 96:128], onesq[:], atb[:, 0:32],
                             start=True, stop=False)
                return e.matmul(ps[5][0:128, 96:128], tri[:], atb[:, 32:64],
                                start=False, stop=True)
            sch.op("pe", mmacs1, r=[(T, "at", c, 0), (T, "at", c, 1), (T, "tri"),
                                    (T, "onesq"), (T, "acs", c, 0)],
                   w=[("psum", 5)], tag=f"{T}acs1_{c}")
            sch.op("dve", (lambda c=c: lambda e:
                           e.tensor_copy(acs[:, c * 64 + 32:c * 64 + 64], ps[5][0:128, 96:128]))(),
                   r=[("psum", 5)], w=[(T, "acs", c, 1)], tag=f"{T}acsc1_{c}")

            def mmal(e, c=c):
                i = e.matmul(ps[5][0:1, 32:64], onescolf[:], atb[:, 0:32],
                             start=True, stop=False)
                return e.matmul(ps[5][0:1, 32:64], onescolf[:], atb[:, 32:64],
                                start=False, stop=True)
            sch.op("pe", mmal, r=[(T, "at", c, 0), (T, "at", c, 1), (T, "onec"),
                                  (T, "acs", c, 1)],
                   w=[("psum", 5)], tag=f"{T}al{c}")
            sch.op("dve", (lambda c=c: lambda e:
                           e.tensor_copy(alast[0:1, c * 32:(c + 1) * 32], ps[5][0:1, 32:64]))(),
                   r=[("psum", 5)], w=[(T, "alast", c)], tag=f"{T}alc{c}")
            sch.op("dve", (lambda c=c: lambda e:
                           e.tensor_tensor(rdec[:], rdec[:], alast[0:1, c * 32:(c + 1) * 32],
                                           OP.add))(),
                   r=[(T, "alast", c), (T, "rdecr")], w=[(T, "rdecr")], tag=f"{T}rda{c}")
            sch.op("pe", (lambda c=c: lambda e:
                          e.matmul(ps[5][0:128, 96:128], cx.onesrow[0:1, 0:128],
                                   alast[0:1, c * 32:(c + 1) * 32], start=True, stop=True))(),
                   r=[(T, "alast", c), ("ones",), (T, "acs", c, 1)], w=[("psum", 5)],
                   tag=f"{T}albm{c}")
            sch.op("dve", (lambda c=c: lambda e:
                           e.tensor_copy(albc[:], ps[5][0:128, 96:128]))(),
                   r=[("psum", 5)], w=[(T, "albc", c)], tag=f"{T}albc{c}")
            for sig in range(2):
                dss = dsb[:, c * 64 + sig * 32:c * 64 + (sig + 1) * 32]
                sch.op("dve", (lambda dss=dss, c=c, sig=sig: lambda e:
                               e.tensor_tensor(dss, albc[:],
                                               acs[:, c * 64 + sig * 32:c * 64 + (sig + 1) * 32],
                                               OP.subtract))(),
                       r=[(T, "albc", c), (T, "acs", c, sig)], w=[(T, "dsd", c, sig)],
                       tag=f"{T}dss{c}_{sig}")
                sch.op("act", (lambda dss=dss, c=c, sig=sig: lambda e:
                               e.activation(dss, dss, AF.Exp))(),
                       r=[(T, "dsd", c, sig)], w=[(T, "ds", c, sig)], tag=f"{T}dse{c}_{sig}")
            sch.op("act", (lambda c=c: lambda e:
                           e.activation(alrow[:], alast[0:1, c * 32:(c + 1) * 32], AF.Exp))(),
                   r=[(T, "alast", c)], w=[(T, "alrow", c)], tag=f"{T}ale{c}")
            sch.op("pe", (lambda c=c: lambda e:
                          e.matmul(ps[5][0:64, 32:64], cx.onesrow[0:1, 0:64], alrow[:],
                                   start=True, stop=True))(),
                   r=[(T, "alrow", c), ("ones",), (T, "alast", c)], w=[("psum", 5)],
                   tag=f"{T}dcm{c}")
            sch.op("dve", (lambda c=c: lambda e:
                           e.tensor_copy(dcol[:], ps[5][0:64, 32:64]))(),
                   r=[("psum", 5)], w=[(T, "dcol", c)], tag=f"{T}dcc{c}")

            xkeys = [(T, "xsil", c % 2, m) for m in range(32)]
            sch.dma("act", (lambda xs_c=xs_c: lambda e:
                            e.dma_start_transpose(
                                out=xtr[:].rearrange("s (h p) -> s h p", h=64),
                                in_=xs_c[:, 0:32 * CHK].rearrange("p (h s) -> p h s", h=64)))(),
                    r=xkeys, w=[(T, "xtr")], tag=f"{T}xtr{c}")
            for h in range(NH):
                for sig in range(2):
                    sl = slice((h * 2 + sig) * 128, (h * 2 + sig) * 128 + 128)
                    t2 = c * 2 + sig
                    sch.op("dve", (lambda sl=sl, t2=t2, h=h: lambda e:
                                   e.tensor_scalar(xst[:, sl], xtr[:, sl],
                                                   dts[:, t2 * 32 + h:t2 * 32 + h + 1],
                                                   None, OP.mult))(),
                           r=[(T, "xtr"), (T, "dts", t2)], w=[(T, "xst", h, sig)],
                           tag=f"{T}xs{c}_{h}_{sig}")
                    sch.op("pool", (lambda sl=sl, c=c, sig=sig, h=h: lambda e:
                                    e.tensor_scalar(xdd[:, sl], xst[:, sl],
                                                    dsb[:, c * 64 + sig * 32 + h:
                                                         c * 64 + sig * 32 + h + 1],
                                                    None, OP.mult))(),
                           r=[(T, "xst", h, sig), (T, "ds", c, sig)],
                           w=[(T, "xdd", h, sig)], tag=f"{T}xd{c}_{h}_{sig}")
            sch.dma("act", (lambda c=c: lambda e:
                            e.dma_start(out=outs["xs"][c],
                                        in_=xst[:]))(),
                    r=[(T, "xst", h, sig) for h in range(NH) for sig in range(2)],
                    w=[("dram", "xs", c)], tag=f"{T}sxs{c}")
            sch.dma("act", (lambda xs_c=xs_c: lambda e:
                            e.dma_start_transpose(
                                out=btb[:].rearrange("s (g n) -> s g n", g=2),
                                in_=xs_c[0:64, 32 * CHK:33 * CHK]
                                    .rearrange("n (g s) -> n g s", g=2)))(),
                    r=[(T, "xsil", c % 2, 32)], w=[(T, "btb")], tag=f"{T}btr{c}")
            Sold = Sst[c % 2]
            Snew = Sst[(c + 1) % 2]
            sch.dma("pool", (lambda c=c, Sold=Sold: lambda e:
                             e.dma_start(out=outs["pref"][c], in_=Sold[:]))(),
                    r=[(T, "S")], w=[("dram", "pref", c)], tag=f"{T}spref{c}")
            for h in range(NH):
                bank = 6 + h % 2

                def mmst(e, h=h, bank=bank):
                    i = e.matmul(ps[bank][0:64, 0:128], btb[:, 0:64],
                                 xdd[:, (h * 2) * 128:(h * 2) * 128 + 128],
                                 start=True, stop=False)
                    return e.matmul(ps[bank][0:64, 0:128], btb[:, 64:128],
                                    xdd[:, (h * 2 + 1) * 128:(h * 2 + 1) * 128 + 128],
                                    start=False, stop=True)
                sch.op("pe", mmst, r=[(T, "btb"), (T, "xdd", h, 0), (T, "xdd", h, 1)],
                       w=[("psum", bank)], tag=f"{T}mst{c}_{h}")
                sch.op("dve", (lambda h=h, bank=bank, Sold=Sold, Snew=Snew: lambda e:
                               e.scalar_tensor_tensor(Snew[:, h * 128:(h + 1) * 128],
                                                      Sold[:, h * 128:(h + 1) * 128],
                                                      dcol[:, h:h + 1],
                                                      ps[bank][0:64, 0:128], OP.mult, OP.add))(),
                       r=[("psum", bank), (T, "S"), (T, "dcol", c)],
                       w=[(T, "Snew", c, h), (T, "S")], tag=f"{T}scan{c}_{h}")
            sch.op("dve", lambda e: e.engine_nop(),
                   r=[(T, "Snew", c, h) for h in range(NH)], w=[(T, "S")],
                   tag=f"{T}sflip{c}")

    sch.dma("pool", lambda e: e.dma_start(out=outs["sfin"], in_=Sst[nchunk % 2][:]),
            r=[(T, "S")], w=[("dram", "sfin")], tag=T + "ssf")
    sch.dma("pool", lambda e: e.dma_start(out=outs["rdec"], in_=rdec[:]),
            r=[(T, "rdecr")], w=[("dram", "rdec")], tag=T + "srd")
    sch.dma("pool", lambda e: e.dma_start(out=outs["acs"], in_=acs[:]),
            r=[(T, "acs", c2, s2) for c2 in range(nchunk) for s2 in range(2)],
            w=[("dram", "acs")], tag=T + "sacs")
    sch.dma("pool", lambda e: e.dma_start(out=outs["alast"], in_=alast[:]),
            r=[(T, "alast", c2) for c2 in range(nchunk)], w=[("dram", "alast")], tag=T + "sal")


def build_p2b(cx, W, ins, x1_dram, x1_name, x2_dram, x2_name, ntok, tag="b"):
    nc, sch = cx.nc, cx.sch
    T = tag
    nchunk = ntok // CHK
    ps = cx.ps
    HB = 4

    sel = cx.sb(T + "sel", [32, DI], F32)
    maskA = cx.sb(T + "maskA", [128, 256], F32)
    identf = cx.sb(T + "id", [128, 128], F32)
    lnwc = cx.sb(T + "lnwc", [128, 32], F32)
    lnbc = cx.sb(T + "lnbc", [128, 32], F32)
    lnbcb = cx.sb(T + "lnbcb", [128, 32], BF16)
    opbr = cx.sb(T + "opb", [1, D], F32)
    oBr = cx.sb(T + "oB", [1, D], F32)
    wsf = [cx.sb(T + "wsf", [128, D], F32) for _ in range(1)]
    wsb = [cx.sb(T + "wsb", [128, D], BF16) for _ in range(2)]
    wst = cx.sb(T + "wst", [128, D], BF16)
    acsb = [cx.sb(T + "acs", [128, 64], F32) for _ in range(2)]
    acm = cx.sb(T + "acm", [32, 256], F32)
    alrow = cx.sb(T + "alast", [1, nchunk * 32], F32)
    alpre = cx.sb(T + "alpre", [1, 32], F32)
    edrow = cx.sb(T + "edrow", [1, 32], F32)
    dpcol = cx.sb(T + "dpcol", [64, 32], F32)
    Sin = cx.sb(T + "Sin", [64, DI], F32)
    prefb = cx.sb(T + "pref", [64, DI], F32)
    sgb = cx.sb(T + "sgb", [64, DI], BF16)
    xsb = cx.sb(T + "xsb", [128, 8192], BF16)
    bcbB = [cx.sb(T + "bcbB", [64, CHK], BF16) for _ in range(2)]
    bcbC = [cx.sb(T + "bcbC", [64, CHK], BF16) for _ in range(2)]
    btc0 = cx.sb(T + "btc0", [128, 256], F32)
    btc1 = cx.sb(T + "btc1", [128, 128], F32)
    t1b = cx.sb(T + "t1b", [128, HB * 384], F32)
    exb = cx.sb(T + "exb", [128, HB * 384], BF16)
    gtb = cx.sb(T + "gtb", [128, HB * 384], BF16)
    ebuf = cx.sb(T + "ebuf", [128, 64], F32)
    ybig = [cx.sb(T + "ybig", [128, DI], BF16) for _ in range(2)]
    ztl = cx.sb(T + "ztl", [128, DI], BF16)
    szb = cx.sb(T + "szb", [128, DI], BF16)
    yf = [cx.sb(T + "yf", [128, DI], BF16) for _ in range(2)]
    mu2 = cx.sb(T + "mu2", [128, 2 * nchunk], F32)
    var2 = cx.sb(T + "var2", [128, 2 * nchunk], F32)
    rstd2 = cx.sb(T + "rstd2", [128, 2 * nchunk], F32)
    x2t = [cx.sb(T + "x2t", [128, D], F32) for _ in range(2)]
    onescolf = cx.sb(T + "onescol", [128, 1], F32)

    for nm, dst, src in [("sel", sel, W["sel"]), ("maskA", maskA, W["maskA"]),
                         ("id", identf, W["identf"]), ("lnwc", lnwc, W["lnwc"]),
                         ("lnbc", lnbc, W["lnbc"]), ("opb", opbr, W["opb"]),
                         ("alast", alrow, ins["alast"])]:
        sch.dma("sp", (lambda dst=dst, src=src: lambda e: e.dma_start(out=dst[:], in_=src))(),
                w=[(T, nm)], tag=T + "ld" + nm)
    sch.op("dve", lambda e: e.memset(onescolf[:], 1.0), w=[(T, "onec")], tag=T + "onec")
    sch.op("dve", lambda e: e.memset(alpre[:], 0.0), w=[(T, "alpre")], tag=T + "alp0")
    sch.op("dve", lambda e: e.memset(var2[:], 1.0), w=[(T, "v2i")], tag=T + "v2i")
    sch.op("dve", lambda e: e.tensor_copy(lnbcb[:], lnbc[:]),
           r=[(T, "lnbc")], w=[(T, "lnbcb")], tag=T + "lnbcb")

    for kc in range(32):
        wf = wsf[0]
        wb2 = wsb[kc % 2]
        sch.dma("sp", (lambda kc=kc, wf=wf: lambda e:
                       e.dma_start(out=wf[:], in_=W["wops"][kc]))(),
                r=[("wops",)], w=[(T, "wsf", 0)], tag=f"{T}lwo{kc}")
        sch.op("dve", (lambda kc=kc, wf=wf, wb2=wb2: lambda e:
                       e.tensor_scalar(wb2[:], wf[:], lnwc[:, kc:kc + 1], None, OP.mult))(),
               r=[(T, "wsf", 0), (T, "lnwc")], w=[(T, "wsb", kc % 2)],
               tag=f"{T}wsc{kc}")
        sch.dma("act", (lambda kc=kc, wb2=wb2: lambda e:
                        e.dma_start(out=W["wopsb"][kc], in_=wb2[:]))(),
                r=[(T, "wsb", kc % 2)], w=[("dram", "wopsb", kc)], tag=f"{T}swo{kc}")
        sch.op("pool", (lambda kc=kc, wf=wf: lambda e:
                        e.tensor_copy(wst[:], wf[:]))(),
               r=[(T, "wsf", 0)], w=[(T, "wst")], tag=f"{T}wcp{kc}")

        def mmob(e, kc=kc):
            i = None
            for fb in range(4):
                off = 32 * (fb // 2)
                i = e.matmul(ps[6 + fb % 2][off:off + 1, 0:512],
                             lnbcb[:, kc:kc + 1],
                             wst[:, fb * 512:(fb + 1) * 512],
                             start=(kc == 0), stop=(kc == 31))
            return i
        sch.op("pe", mmob, r=[(T, "wst"), (T, "lnbcb")],
               w=[("psum", 6), ("psum", 7)] if kc in (0, 31) else [], tag=f"{T}mob{kc}")
    for fb in range(4):
        sch.op("dve", (lambda fb=fb: lambda e:
                       e.tensor_tensor(oBr[0:1, fb * 512:(fb + 1) * 512],
                                       ps[6 + fb % 2][32 * (fb // 2):32 * (fb // 2) + 1, 0:512],
                                       opbr[0:1, fb * 512:(fb + 1) * 512], OP.add))(),
               r=[("psum", 6), ("psum", 7), (T, "opb")], w=[(T, "oB", fb)], tag=f"{T}ob{fb}")
    oB_keys = [(T, "oB", fb) for fb in range(4)]

    sch.op("dve", lambda e: e.memset(Sin[:], 0.0), w=[(T, "Sin")], tag=T + "sin0")
    for j in range(3):
        sch.dma("sp", (lambda j=j: lambda e:
                       e.dma_start(out=prefb[:], in_=ins["pstate"][j]))(),
                r=[("pstate",)], w=[(T, "prefb")], tag=f"{T}lps{j}")
        sch.dma("sp", (lambda j=j: lambda e:
                       e.dma_start(out=edrow[:], in_=ins["pdec"][j:j + 1, :]))(),
                r=[("pdec",)], w=[(T, "edrow")], tag=f"{T}lpd{j}")
        sch.op("act", lambda e: e.activation(edrow[:], edrow[:], AF.Exp),
               r=[(T, "edrow")], w=[(T, "edrow")], tag=f"{T}pde{j}")
        sch.op("pe", lambda e: e.matmul(ps[5][0:64, 0:32], cx.onesrow[0:1, 0:64],
                                        edrow[:], start=True, stop=True),
               r=[(T, "edrow"), ("ones",)], w=[("psum", 5)], tag=f"{T}pdm{j}")
        sch.op("dve", lambda e: e.tensor_copy(dpcol[:], ps[5][0:64, 0:32]),
               r=[("psum", 5)], w=[(T, "dpcol")], tag=f"{T}pdc{j}")
        for h in range(NH):
            sch.op("dve", (lambda h=h: lambda e:
                           e.scalar_tensor_tensor(Sin[:, h * 128:(h + 1) * 128],
                                                  Sin[:, h * 128:(h + 1) * 128],
                                                  dpcol[:, h:h + 1],
                                                  prefb[:, h * 128:(h + 1) * 128],
                                                  OP.mult, OP.add))(),
                   r=[(T, "Sin"), (T, "dpcol"), (T, "prefb")], w=[(T, "Sin")],
                   tag=f"{T}sf{j}_{h}")

    for c in range(nchunk):
        lam2 = [c * 2, c * 2 + 1]
        ab = acsb[c % 2]
        sch.dma("sp", (lambda c=c, ab=ab: lambda e:
                       e.dma_start(out=ab[:], in_=ins["acs"][:, c * 64:(c + 1) * 64]))(),
                r=[("dram", "acs")], w=[(T, "acsb", c % 2)], tag=f"{T}lac{c}")
        sch.dma("sp", (lambda c=c: lambda e:
                       e.dma_start(out=xsb[:], in_=ins["xs"][c]))(),
                r=[("dram", "xs", c), (T, "xsfree")], w=[(T, "xsb")], tag=f"{T}lxs{c}")
        bclB = bcbB[c % 2]
        bclC = bcbC[c % 2]
        sch.dma("sp", (lambda c=c, bclB=bclB: lambda e:
                       e.dma_start(out=bclB[:], in_=ins["bc"][0:64, c * CHK:(c + 1) * CHK]))(),
                r=[("dram", "bc", c)], w=[(T, "bcbB", c % 2)], tag=f"{T}lbcB{c}")
        sch.dma("sp", (lambda c=c, bclC=bclC: lambda e:
                       e.dma_start(out=bclC[:], in_=ins["bc"][64:128, c * CHK:(c + 1) * CHK]))(),
                r=[("dram", "bc", c)], w=[(T, "bcbC", c % 2)], tag=f"{T}lbcC{c}")
        sch.dma("sp", (lambda c=c: lambda e:
                       e.dma_start(out=prefb[:], in_=ins["pref"][c]))(),
                r=[("dram", "pref", c), (T, "preffree")], w=[(T, "prefb")], tag=f"{T}lpr{c}")

        sch.op("act", (lambda: lambda e: e.activation(edrow[:], alpre[:], AF.Exp))(),
               r=[(T, "alpre"), (T, "edrow")], w=[(T, "edrow")], tag=f"{T}dpe{c}")
        sch.op("pe", (lambda: lambda e: e.matmul(ps[5][0:64, 0:32], cx.onesrow[0:1, 0:64],
                                                 edrow[:], start=True, stop=True))(),
               r=[(T, "edrow"), ("ones",)], w=[("psum", 5)], tag=f"{T}dpm{c}")
        sch.op("dve", (lambda: lambda e: e.tensor_copy(dpcol[:], ps[5][0:64, 0:32]))(),
               r=[("psum", 5)], w=[(T, "dpcol")], tag=f"{T}dpc{c}")
        for h in range(NH):
            sch.op("dve", (lambda h=h: lambda e:
                           e.scalar_tensor_tensor(prefb[:, h * 128:(h + 1) * 128],
                                                  Sin[:, h * 128:(h + 1) * 128],
                                                  dpcol[:, h:h + 1],
                                                  prefb[:, h * 128:(h + 1) * 128],
                                                  OP.mult, OP.add))(),
                   r=[(T, "Sin"), (T, "dpcol"), (T, "prefb")], w=[(T, "prefb")],
                   tag=f"{T}sg{c}_{h}")
        sch.op("dve", (lambda: lambda e: e.tensor_copy(sgb[:], prefb[:]))(),
               r=[(T, "prefb")], w=[(T, "sgb"), (T, "preffree")], tag=f"{T}sgc{c}")
        sch.op("dve", (lambda c=c: lambda e:
                       e.tensor_tensor(alpre[:], alpre[:], alrow[0:1, c * 32:(c + 1) * 32],
                                       OP.add))(),
               r=[(T, "alast"), (T, "edrow")], w=[(T, "alpre")], tag=f"{T}alp{c}")

        def mmtr(e, ab=ab):
            i = e.transpose(ps[5][0:32, 128:256], ab[:, 0:32], identf[:])
            return e.transpose(ps[5][0:32, 256:384], ab[:, 32:64], identf[:])
        sch.op("pe", mmtr, r=[(T, "acsb", c % 2), (T, "id")], w=[("psum", 5)],
               tag=f"{T}acmt{c}")
        sch.op("dve", (lambda: lambda e: e.tensor_copy(acm[:], ps[5][0:32, 128:384]))(),
               r=[("psum", 5)], w=[(T, "acm")], tag=f"{T}acmc{c}")
        sch.op("act", (lambda ab=ab: lambda e: e.activation(ebuf[:], ab[:], AF.Exp))(),
               r=[(T, "acsb", c % 2)], w=[(T, "ebuf")], tag=f"{T}ebe{c}")

        def mmbtc(e, bclB=bclB, bclC=bclC):
            e.matmul(ps[4][0:128, 0:256], bclB[:, 0:128], bclC[:, 0:256],
                     start=True, stop=True)
            return e.matmul(ps[4][0:128, 256:384], bclB[:, 128:256],
                            bclC[:, 128:256], start=True, stop=True)
        sch.op("pe", mmbtc, r=[(T, "bcbB", c % 2), (T, "bcbC", c % 2)], w=[("psum", 4)], tag=f"{T}btc{c}")
        sch.op("dve", (lambda: lambda e:
                       e.tensor_tensor(btc0[:], ps[4][0:128, 0:256], maskA[:], OP.mult))(),
               r=[("psum", 4), (T, "maskA")], w=[(T, "btc0")], tag=f"{T}btm0{c}")
        sch.op("dve", (lambda: lambda e:
                       e.tensor_tensor(btc1[:], ps[4][0:128, 256:384], maskA[:, 0:128],
                                       OP.mult))(),
               r=[("psum", 4), (T, "maskA")], w=[(T, "btc1")], tag=f"{T}btm1{c}")

        for hb in range(NH // HB):
            for hh in range(HB):
                h = hb * HB + hh
                def mmrb(e, h=h):
                    e.matmul(ps[5][0:128, 128:384], sel[:, h * 128:(h + 1) * 128],
                             acm[:, 0:256], start=True, stop=True)
                    return e.matmul(ps[7][0:128, 0:128], sel[:, h * 128:(h + 1) * 128],
                                    acm[:, 128:256], start=True, stop=True)
                sch.op("pe", mmrb, r=[(T, "acm"), (T, "sel")],
                       w=[("psum", 5), ("psum", 7)], tag=f"{T}rb{c}_{h}")
                o0 = hh * 384
                sch.op("dve", (lambda h=h, o0=o0, ab=ab: lambda e:
                               e.tensor_scalar(t1b[:, o0:o0 + 256], ps[5][0:128, 128:384],
                                               ab[:, h:h + 1], 0.0, OP.subtract, OP.min))(),
                       r=[("psum", 5), (T, "acsb", c % 2)], w=[(T, "t1", hh)],
                       tag=f"{T}t10_{c}_{h}")
                sch.op("dve", (lambda h=h, o0=o0, ab=ab: lambda e:
                               e.tensor_scalar(t1b[:, o0 + 256:o0 + 384], ps[7][0:128, 0:128],
                                               ab[:, 32 + h:32 + h + 1], 0.0,
                                               OP.subtract, OP.min))(),
                       r=[("psum", 7), (T, "acsb", c % 2)], w=[(T, "t1b", hh)],
                       tag=f"{T}t11_{c}_{h}")
            sch.op("act", (lambda: lambda e: e.activation(exb[:], t1b[:], AF.Exp))(),
                   r=[(T, "t1", hh2) for hh2 in range(HB)] +
                     [(T, "t1b", hh2) for hh2 in range(HB)],
                   w=[(T, "exb")], tag=f"{T}exp{c}_{hb}")
            for hh in range(HB):
                h = hb * HB + hh
                o0 = hh * 384
                sch.op("dve", (lambda o0=o0: lambda e:
                               e.tensor_tensor(gtb[:, o0:o0 + 256], exb[:, o0:o0 + 256],
                                               btc0[:], OP.mult))(),
                       r=[(T, "exb"), (T, "btc0")], w=[(T, "gt", hh, 0)],
                       tag=f"{T}gt0_{c}_{h}")
                sch.op("dve", (lambda o0=o0: lambda e:
                               e.tensor_tensor(gtb[:, o0 + 256:o0 + 384],
                                               exb[:, o0 + 256:o0 + 384], btc1[:], OP.mult))(),
                       r=[(T, "exb"), (T, "btc1")], w=[(T, "gt", hh, 1)],
                       tag=f"{T}gt1_{c}_{h}")
                bA = 2 * (hh % 2)
                bB = bA + 1

                def mmy(e, h=h, o0=o0, bA=bA, bB=bB, c=c):
                    e.matmul(ps[bB][0:128, 0:128], bcbC[c % 2][:, 0:128],
                             sgb[:, h * 128:(h + 1) * 128], start=True, stop=True)
                    e.matmul(ps[bB][0:128, 128:256], bcbC[c % 2][:, 128:256],
                             sgb[:, h * 128:(h + 1) * 128], start=True, stop=True)
                    e.matmul(ps[bA][0:128, 0:128], gtb[:, o0:o0 + 128],
                             xsb[:, (2 * h) * 128:(2 * h) * 128 + 128],
                             start=True, stop=True)
                    e.matmul(ps[bA][0:128, 128:256], gtb[:, o0 + 128:o0 + 256],
                             xsb[:, (2 * h) * 128:(2 * h) * 128 + 128],
                             start=True, stop=False)
                    return e.matmul(ps[bA][0:128, 128:256], gtb[:, o0 + 256:o0 + 384],
                                    xsb[:, (2 * h + 1) * 128:(2 * h + 1) * 128 + 128],
                                    start=False, stop=True)
                sch.op("pe", mmy, r=[(T, "gt", hh, 0), (T, "gt", hh, 1), (T, "xsb"),
                                     (T, "sgb"), (T, "bcbC", c % 2)],
                       w=[("psum", bA), ("psum", bB)], tag=f"{T}y{c}_{h}")
                for lam in range(2):
                    yb = ybig[lam]
                    sch.op("act", (lambda h=h, lam=lam, bB=bB, yb=yb: lambda e:
                                   e.activation(yb[:, h * 128:(h + 1) * 128],
                                                ps[bB][0:128, lam * 128:lam * 128 + 128],
                                                AF.Copy,
                                                scale=ebuf[:, lam * 32 + h:lam * 32 + h + 1]))(),
                           r=[("psum", bB), (T, "ebuf")], w=[(T, "ybo", lam, h)],
                           tag=f"{T}ye{c}_{h}_{lam}")
                    sch.op("dve", (lambda h=h, lam=lam, bA=bA, yb=yb: lambda e:
                                   e.tensor_tensor(yb[:, h * 128:(h + 1) * 128],
                                                   yb[:, h * 128:(h + 1) * 128],
                                                   ps[bA][0:128, lam * 128:lam * 128 + 128],
                                                   OP.add))(),
                           r=[("psum", bA), (T, "ybo", lam, h)], w=[(T, "yb", lam, h)],
                           tag=f"{T}ya{c}_{h}_{lam}")
        sch.op("dve", lambda e: e.engine_nop(),
               r=[(T, "yb", lam, h2) for lam in range(2) for h2 in range(NH)],
               w=[(T, "xsfree")], tag=f"{T}xsf{c}")

        for lam in range(2):
            t = lam2[lam]
            yb = ybig[lam]
            sch.dma("sp", (lambda t=t: lambda e:
                           e.dma_start(out=ztl[:], in_=ins["z"][t * 128:(t + 1) * 128, :]))(),
                    r=[("dram", "z", t // 4, t % 4, zb2) for zb2 in range(8)] +
                      [(T, "ztlfree")],
                    w=[(T, "ztl")], tag=f"{T}lz{t}")
            sch.op("act", (lambda: lambda e: e.activation(szb[:], ztl[:], AF.Silu))(),
                   r=[(T, "ztl")], w=[(T, "szb"), (T, "ztlfree")], tag=f"{T}sz{t}")
            sch.op("dve", (lambda yb=yb: lambda e:
                           e.tensor_tensor(yb[:], yb[:], szb[:], OP.mult))(),
                   r=[(T, "yb", lam, h2) for h2 in range(NH)] + [(T, "szb")],
                   w=[(T, "g", lam)], tag=f"{T}gm{t}")
            sch.op("dve", (lambda yb=yb, t=t: lambda e:
                            e.reduce_sum(cx.sumcol[:, t % 64:t % 64 + 1], yb[:], axis=AX.X))(),
                   r=[(T, "g", lam)], w=[(T, "gsum", t)], tag=f"{T}gs{t}")
            sch.op("act", (lambda yb=yb, t=t: lambda e:
                           e.activation(szb[:], yb[:], AF.Square,
                                        accum_out=cx.sumsqcol[:, t % 64:t % 64 + 1]))(),
                   r=[(T, "g", lam), (T, "gsum", t)], w=[(T, "gssq", t), (T, "szb")],
                   tag=f"{T}gq{t}")
            sch.op("dve", (lambda t=t: lambda e:
                           e.tensor_scalar(mu2[:, t:t + 1], cx.sumcol[:, t % 64:t % 64 + 1],
                                           1.0 / DI, None, OP.mult))(),
                   r=[(T, "gsum", t)], w=[(T, "gmu", t)], tag=f"{T}gmu{t}")
            sch.op("dve", (lambda t=t: lambda e:
                           e.tensor_tensor(var2[:, t:t + 1], mu2[:, t:t + 1],
                                           mu2[:, t:t + 1], OP.mult))(),
                   r=[(T, "gmu", t)], w=[(T, "gm2", t)], tag=f"{T}gm2{t}")
            sch.op("dve", (lambda t=t: lambda e:
                           e.scalar_tensor_tensor(var2[:, t:t + 1],
                                                  cx.sumsqcol[:, t % 64:t % 64 + 1], 1.0 / DI,
                                                  var2[:, t:t + 1], OP.mult, OP.subtract))(),
                   r=[(T, "gssq", t), (T, "gm2", t)], w=[(T, "gvp", t)], tag=f"{T}gv{t}")
            sch.op("dve", (lambda t=t: lambda e:
                           e.tensor_scalar(var2[:, t:t + 1], var2[:, t:t + 1], EPS,
                                           None, OP.add))(),
                   r=[(T, "gvp", t), (T, "v2i")], w=[(T, "gve", t)], tag=f"{T}gvee{t}")
            sch.op("act", (lambda t=t: lambda e:
                           e.activation(rstd2[:, t:t + 1], var2[:, t:t + 1], AF.Sqrt))(),
                   r=[(T, "gve", t)], w=[(T, "gsd", t)], tag=f"{T}gsq{t}")
            sch.op("dve", (lambda t=t: lambda e:
                           e.reciprocal(rstd2[:, t:t + 1], rstd2[:, t:t + 1]))(),
                   r=[(T, "gsd", t)], w=[(T, "grs", t)], tag=f"{T}gri{t}")
            sch.op("dve", (lambda yb=yb, t=t: lambda e:
                           e.tensor_scalar(yb[:], yb[:], mu2[:, t:t + 1],
                                           rstd2[:, t:t + 1], OP.subtract, OP.mult))(),
                   r=[(T, "g", lam), (T, "gmu", t), (T, "grs", t)], w=[(T, "gn", lam)],
                   tag=f"{T}gn{t}")
            yft = yf[lam]
            sch.dma("act", (lambda yb=yb, yft=yft: lambda e:
                            e.dma_start_transpose(
                                out=yft[:].rearrange("p (k t) -> p k t", k=32),
                                in_=yb[:].rearrange("t (k p) -> t k p", k=32)))(),
                    r=[(T, "gn", lam)], w=[(T, "yf", lam)], tag=f"{T}ytr{t}")
            x2b = x2t[lam]
            sch.dma("act", (lambda x2b=x2b, t=t: lambda e:
                            e.dma_start(out=x2b[:],
                                        in_=x1_dram[t * 128:(t + 1) * 128, :]))(),
                    r=dram_tile_keys(x1_name, t * 128, 128) + [(T, "x2free", lam)],
                    w=[(T, "x2b", lam)], tag=f"{T}lx1{t}")

        for fb in range(4):
            f0 = fb * 512
            for kc in range(32):
                wb2 = wsb[(kc // 4) % 2]
                if kc % 4 == 0:
                    sch.dma("sp", (lambda kc=kc, f0=f0, wb2=wb2: lambda e:
                                   e.dma_start(out=wb2[:].rearrange("p (q f) -> p q f", q=4),
                                               in_=W["wopsb"][kc:kc + 4, :, f0:f0 + 512]
                                               .rearrange("q p f -> p q f")))(),
                            r=[("dram", "wopsb", kc + qq) for qq in range(4)],
                            w=[(T, "wsb", (kc // 4) % 2)],
                            tag=f"{T}lws{c}_{fb}_{kc}")

                def mmo(e, kc=kc, f0=f0, wb2=wb2, fb=fb):
                    i = None
                    woff = (kc % 4) * 512
                    for lam in range(2):
                        bank = fb % 2 * 2 + lam
                        if kc == 0:
                            i = e.matmul(ps[bank][0:128, 0:512], cx.onesrow[0:1, 0:128],
                                         oBr[0:1, f0:f0 + 512], start=True, stop=False)
                        i = e.matmul(ps[bank][0:128, 0:512],
                                     yf[lam][:, kc * 128:(kc + 1) * 128],
                                     wb2[:, woff:woff + 512],
                                     start=False, stop=(kc == 31))
                    return i
                wkeys = ([("psum", fb % 2 * 2), ("psum", fb % 2 * 2 + 1)]
                         if kc in (0, 31) else [])
                sch.op("pe", mmo,
                       r=[(T, "wsb", (kc // 4) % 2), (T, "yf", 0), (T, "yf", 1), ("ones",)] + oB_keys,
                       w=wkeys, tag=f"{T}mo{c}_{fb}_{kc}")
            for lam in range(2):
                t = lam2[lam]
                x2b = x2t[lam]
                bank = fb % 2 * 2 + lam
                sch.op("dve", (lambda x2b=x2b, bank=bank, f0=f0: lambda e:
                               e.tensor_tensor(x2b[:, f0:f0 + 512], ps[bank][0:128, 0:512],
                                               x2b[:, f0:f0 + 512], OP.add))(),
                       r=[("psum", bank), (T, "x2b", lam)], w=[(T, "x2d", lam, fb)],
                       tag=f"{T}xa{t}_{fb}")
        for lam in range(2):
            t = lam2[lam]
            x2b = x2t[lam]
            sch.dma("act", (lambda x2b=x2b, t=t: lambda e:
                            e.dma_start(out=x2_dram[t * 128:(t + 1) * 128, :], in_=x2b[:]))(),
                    r=[(T, "x2d", lam, fb2) for fb2 in range(4)],
                    w=dram_tile_keys(x2_name, t * 128, 128) + [(T, "x2free", lam)],
                    tag=f"{T}sx2{t}")


def emit_weight_prep(cx, pairs):
    sch = cx.sch
    for nm, src, dst, nparts in pairs:
        for i in range(nparts):
            sch.dma("pool", (lambda src=src, dst=dst, i=i: lambda e:
                             e.dma_start(out=dst[i], in_=src[i]))(),
                    w=[("dram", nm, i)], tag=f"prep_{nm}_{i}")


def declare_l1(nc, ntok):
    nchunk = ntok // CHK
    EI, EO = "ExternalInput", "ExternalOutput"
    t = {}
    t["u"] = nc.dram_tensor("u", [ntok + 3, D], F32, kind=EI).ap()
    t["w0s"] = nc.dram_tensor("w0s", [64, 128, 16, 128], F32, kind=EI).ap()
    t["w1s"] = nc.dram_tensor("w1s", [64, 128, 16, 128], F32, kind=EI).ap()
    t["b0"] = nc.dram_tensor("b0", [1, DH], F32, kind=EI).ap()
    t["b1"] = nc.dram_tensor("b1", [1, D], F32, kind=EI).ap()
    for nm, shp in [("wipx", [33, 128, 16, 128]), ("wipz", [8, 4, 128, 2048]),
                    ("wipdt", [128, 16, 32]), ("cwp", [128, 33, 4]), ("cbp", [128, 33]),
                    ("xbcbc", [128, 33]), ("zbias", [8, 512]), ("dtb", [1, 32]),
                    ("dtbias", [1, 32]), ("alog", [1, 32]), ("tri", [128, 128]),
                    ("onesq", [128, 128]), ("hmask", [128, 1])]:
        t[nm] = nc.dram_tensor(nm, shp, F32, kind=EI).ap()
    t["x1"] = nc.dram_tensor("x1", [ntok + 3, D], F32, kind=EO).ap()
    t["o_z"] = nc.dram_tensor("o_z", [ntok, DI], BF16, kind=EO).ap()
    t["o_xs"] = nc.dram_tensor("o_xs", [nchunk, 128, 8192], BF16, kind=EO).ap()
    t["o_bc"] = nc.dram_tensor("o_bc", [128, ntok], BF16, kind=EO).ap()
    t["o_pref"] = nc.dram_tensor("o_pref", [nchunk, 64, DI], F32, kind=EO).ap()
    t["o_acs"] = nc.dram_tensor("o_acs", [128, nchunk * 64], F32, kind=EO).ap()
    t["o_alast"] = nc.dram_tensor("o_alast", [1, nchunk * 32], F32, kind=EO).ap()
    t["o_sfin"] = nc.dram_tensor("o_sfin", [64, DI], F32, kind=EO).ap()
    t["o_rdec"] = nc.dram_tensor("o_rdec", [1, 32], F32, kind=EO).ap()
    return t


def build_l1(ntok):
    nc = bass.Bass("TRN2", target_bir_lowering=False)
    t = declare_l1(nc, ntok)
    nchunk = ntok // CHK
    t["w0bs"] = nc.dram_tensor("w0bs", [64, 128, 16, 128], BF16, kind="Internal").ap()
    t["w1bs"] = nc.dram_tensor("w1bs", [64, 128, 16, 128], BF16, kind="Internal").ap()
    t["wipxs"] = nc.dram_tensor("wipxs", [33, 128, 16, 128], BF16, kind="Internal").ap()
    t["wipzs"] = nc.dram_tensor("wipzs", [8, 4, 128, 2048], BF16, kind="Internal").ap()
    cx = Ctx(nc, ntok)
    sch = cx.sch
    emit_weight_prep(cx, [("w0bs", t["w0s"], t["w0bs"], 64),
                          ("w1bs", t["w1s"], t["w1bs"], 64),
                          ("wipxs", t["wipx"], t["wipxs"], 33)])
    with cx.phase():
        build_ffn_phase(cx, t["w0bs"], t["w1bs"], t["b0"], t["b1"],
                        t["u"], "u", t["x1"], "x1", ntok + 3, "p1")
    for zb in range(8):
        for kcq in range(4):
            sch.dma("pool", (lambda zb=zb, kcq=kcq: lambda e:
                             e.dma_start(out=t["wipzs"][zb, kcq], in_=t["wipz"][zb, kcq]))(),
                    w=[("dram", "wipzs", zb, kcq)], tag=f"prep_wipz_{zb}_{kcq}")
    W = {k: t[k] for k in ["wipx", "wipz", "wipdt", "cwp", "cbp", "xbcbc",
                           "dtb", "dtbias", "alog", "tri", "onesq", "hmask"]}
    W["zb"] = t["zbias"]
    W["wipx"] = t["wipxs"]
    W["wipz"] = t["wipzs"]
    outs = dict(z=t["o_z"], xs=t["o_xs"], bc=t["o_bc"], pref=t["o_pref"],
                acs=t["o_acs"], alast=t["o_alast"], sfin=t["o_sfin"], rdec=t["o_rdec"])
    with cx.phase():
        build_p2a(cx, W, t["x1"], "x1", outs, ntok)
    fin = (dram_tile_keys("x1", 0, ntok + 3) +
           [("dram", "sfin"), ("dram", "acs"), ("dram", "alast"), ("dram", "rdec")] +
           [("dram", "pref", c) for c in range(nchunk)] +
           [("dram", "xs", c) for c in range(nchunk)] +
           [("dram", "bc", c) for c in range(nchunk)] +
           [("dram", "z", g, tt, zb) for g in range(ntok // GRP) for tt in range(4)
            for zb in range(8)])
    sch.op("sp", None, r=fin, tag="fin")
    sch.emit(nc)
    return nc


def declare_l2(nc, ntok, dbg_x2=False):
    nchunk = ntok // CHK
    EI, EO = "ExternalInput", "ExternalOutput"
    t = {}
    t["x1"] = nc.dram_tensor("x1", [ntok + 3, D], F32, kind=EI).ap()
    t["i_z"] = nc.dram_tensor("i_z", [ntok, DI], BF16, kind=EI).ap()
    t["i_xs"] = nc.dram_tensor("i_xs", [nchunk, 128, 8192], BF16, kind=EI).ap()
    t["i_bc"] = nc.dram_tensor("i_bc", [128, ntok], BF16, kind=EI).ap()
    t["i_pref"] = nc.dram_tensor("i_pref", [nchunk, 64, DI], F32, kind=EI).ap()
    t["i_acs"] = nc.dram_tensor("i_acs", [128, nchunk * 64], F32, kind=EI).ap()
    t["i_alast"] = nc.dram_tensor("i_alast", [1, nchunk * 32], F32, kind=EI).ap()
    t["i_pstate"] = nc.dram_tensor("i_pstate", [3, 64, DI], F32, kind=EI).ap()
    t["i_pdec"] = nc.dram_tensor("i_pdec", [3, 32], F32, kind=EI).ap()
    t["wops"] = nc.dram_tensor("wops", [32, 128, D], F32, kind=EI).ap()
    t["wopsb"] = nc.dram_tensor("wopsb", [32, 128, D], BF16, kind="Internal").ap()
    t["lnwc"] = nc.dram_tensor("lnwc", [128, 32], F32, kind=EI).ap()
    t["lnbc"] = nc.dram_tensor("lnbc", [128, 32], F32, kind=EI).ap()
    t["opb"] = nc.dram_tensor("opb", [1, D], F32, kind=EI).ap()
    t["sel"] = nc.dram_tensor("sel", [32, DI], F32, kind=EI).ap()
    t["maskA"] = nc.dram_tensor("maskA", [128, 256], F32, kind=EI).ap()
    t["identf"] = nc.dram_tensor("identf", [128, 128], F32, kind=EI).ap()
    t["w0s"] = nc.dram_tensor("w0s", [64, 128, 16, 128], F32, kind=EI).ap()
    t["w1s"] = nc.dram_tensor("w1s", [64, 128, 16, 128], F32, kind=EI).ap()
    t["b0"] = nc.dram_tensor("b0", [1, DH], F32, kind=EI).ap()
    t["b1"] = nc.dram_tensor("b1", [1, D], F32, kind=EI).ap()
    t["x2"] = nc.dram_tensor("x2", [ntok, D], F32, kind=EO if dbg_x2 else "Internal").ap()
    t["o_out"] = nc.dram_tensor("o_out", [ntok, D], F32, kind=EO).ap()
    return t


def build_l2(ntok, dbg_x2=False):
    nc = bass.Bass("TRN2", target_bir_lowering=False)
    t = declare_l2(nc, ntok, dbg_x2)
    t["w0bs"] = nc.dram_tensor("w0bs", [64, 128, 16, 128], BF16, kind="Internal").ap()
    t["w1bs"] = nc.dram_tensor("w1bs", [64, 128, 16, 128], BF16, kind="Internal").ap()
    cx = Ctx(nc, ntok)
    sch = cx.sch
    emit_weight_prep(cx, [("w0bs", t["w0s"], t["w0bs"], 64),
                          ("w1bs", t["w1s"], t["w1bs"], 64)])
    W = {k: t[k] for k in ["wops", "wopsb", "lnwc", "lnbc", "opb", "sel",
                           "maskA", "identf"]}
    ins = dict(z=t["i_z"], xs=t["i_xs"], bc=t["i_bc"], pref=t["i_pref"],
               acs=t["i_acs"], alast=t["i_alast"], pstate=t["i_pstate"],
               pdec=t["i_pdec"])
    with cx.phase():
        build_p2b(cx, W, ins, t["x1"], "x1", t["x2"], "x2", ntok)
    with cx.phase():
        build_ffn_phase(cx, t["w0bs"], t["w1bs"], t["b0"], t["b1"],
                        t["x2"], "x2", t["o_out"], "o_out", ntok, "p3")
    sch.op("sp", None, r=dram_tile_keys("o_out", 0, ntok), tag="fin")
    sch.emit(nc)
    return nc


def stage_weights_l1(inp):
    w = {}
    w["w0s"] = prep_w0(inp["L0_ffn_W0"])
    w["w1s"] = prep_w1(inp["L0_ffn_W1"])
    w["b0"] = inp["L0_ffn_bias0"].reshape(1, -1).copy()
    w["b1"] = inp["L0_ffn_bias1"].reshape(1, -1).copy()
    Wip = inp["L0_attn_in_proj_w"]
    ipb = inp["L0_attn_in_proj_b"]
    w["wipx"] = np.ascontiguousarray(
        Wip[4096:8320].reshape(33, 128, 16, 128).transpose(0, 3, 2, 1))
    w["wipz"] = np.ascontiguousarray(
        Wip[0:4096].reshape(8, 512, 4, 4, 128).transpose(0, 2, 4, 3, 1)
        .reshape(8, 4, 128, 2048))
    w["wipdt"] = np.ascontiguousarray(
        Wip[8320:8352].reshape(32, 16, 128).transpose(2, 1, 0))
    cw = inp["L0_attn_conv_w"]
    w["cwp"] = np.ascontiguousarray(cw[:4224, 0, :].reshape(33, 128, 4).transpose(1, 0, 2))
    w["cbp"] = np.ascontiguousarray(inp["L0_attn_conv_b"][:4224].reshape(33, 128).T)
    w["xbcbc"] = np.ascontiguousarray(ipb[4096:8320].reshape(33, 128).T)
    w["zbias"] = ipb[0:4096].reshape(8, 512).copy()
    w["dtb"] = ipb[8320:8352].reshape(1, -1).copy()
    w["dtbias"] = inp["L0_attn_dt_bias"].reshape(1, -1).copy()
    w["alog"] = inp["L0_attn_A_log"].reshape(1, -1).copy()
    w["tri"] = np.ascontiguousarray(np.tril(np.ones((128, 128), np.float32)).T)
    w["onesq"] = np.ones((128, 128), np.float32)
    return w


def stage_weights_l2(inp):
    w = {}
    w["wops"] = np.ascontiguousarray(
        inp["L0_attn_out_proj_w"].reshape(D, 32, 128).transpose(1, 2, 0))
    w["lnwc"] = np.ascontiguousarray(inp["L0_attn_ln_w"].reshape(32, 128).T)
    w["lnbc"] = np.ascontiguousarray(inp["L0_attn_ln_b"].reshape(32, 128).T)
    w["opb"] = inp["L0_attn_out_proj_b"].reshape(1, -1).copy()
    w["sel"] = np.kron(np.eye(32, dtype=np.float32), np.ones((1, 128), np.float32))
    w["maskA"] = (np.arange(256)[None, :] >= np.arange(128)[:, None]).astype(np.float32)
    w["identf"] = np.eye(128, dtype=np.float32)
    w["w0s"] = prep_w0(inp["L0_ffn_W0"])
    w["w1s"] = prep_w1(inp["L0_ffn_W1"])
    w["b0"] = inp["L0_ffn_bias0"].reshape(1, -1).copy()
    w["b1"] = inp["L0_ffn_bias1"].reshape(1, -1).copy()
    return w


NTOK = 4096


def kernel(**inputs):
    u = np.ascontiguousarray(inputs["L0_attn_u"], dtype=np.float32)
    B, L, _ = u.shape
    npb = L // NTOK
    w1 = stage_weights_l1(inputs)
    w2 = stage_weights_l2(inputs)
    nc1 = build_l1(NTOK)
    maps1 = []
    for k in range(NCORES):
        b, pos = k // npb, k % npb
        s = pos * NTOK
        us = np.zeros((NTOK + 3, D), np.float32)
        us[:NTOK] = u[b, s:s + NTOK]
        if pos > 0:
            us[NTOK:] = u[b, s - 3:s]
        m = dict(w1)
        m["u"] = us
        m["hmask"] = np.full((128, 1), 0.0 if pos == 0 else 1.0, np.float32)
        maps1.append(m)
    r1 = run_bass_kernel_spmd(nc1, maps1, core_ids=list(range(NCORES)))

    nc2 = build_l2(NTOK)
    maps2 = []
    for k in range(NCORES):
        b, pos = k // npb, k % npb
        o1 = r1.results[k]
        pstate = np.zeros((3, 64, DI), np.float32)
        pdec = np.zeros((3, 32), np.float32)
        for j in range(min(3, pos)):
            pk = k - 1 - j
            pstate[2 - j] = r1.results[pk]["o_sfin"]
            pdec[2 - j] = r1.results[pk]["o_rdec"].reshape(32)
        m = dict(w2)
        m["x1"] = o1["x1"]
        m["i_z"] = o1["o_z"]
        m["i_xs"] = o1["o_xs"]
        m["i_bc"] = o1["o_bc"]
        m["i_pref"] = o1["o_pref"]
        m["i_acs"] = o1["o_acs"]
        m["i_alast"] = o1["o_alast"]
        m["i_pstate"] = pstate
        m["i_pdec"] = pdec
        maps2.append(m)
    r2 = run_bass_kernel_spmd(nc2, maps2, core_ids=list(range(NCORES)))

    out = np.empty((B, L, D), np.float32)
    for k in range(NCORES):
        b, pos = k // npb, k % npb
        out[b, pos * NTOK:(pos + 1) * NTOK] = r2.results[k]["o_out"]
    return out



# version 08 (final)
# speedup vs baseline: 1.1621x; 1.1621x over previous
import math
from contextlib import ExitStack, contextmanager
import numpy as np
import ml_dtypes
import concourse.bass as bass
import concourse.mybir as mybir
from concourse.bass_utils import run_bass_kernel_spmd

F32 = mybir.dt.float32
BF16 = mybir.dt.bfloat16
AF = mybir.ActivationFunctionType
OP = mybir.AluOpType
AX = mybir.AxisListType
BF = ml_dtypes.bfloat16

D = 2048
DI = 4096
DH = 8192
NH = 32
HD = 128
NST = 64
CHK = 256
CONVD = DI + 2 * NST
DIP = 2 * DI + 2 * NST + NH
EPS = 1e-5
GRP = 512
NCORES = 8


def cdiv(a, b):
    return (a + b - 1) // b


class Sched:
    STREAMS = ("pe", "dve", "act", "pool", "sp")

    def __init__(self):
        self.ops = []

    def op(self, stream, fn, r=(), w=(), tag=""):
        self.ops.append(dict(s=stream, fn=fn, r=tuple(r), w=tuple(w),
                             dma=False, tag=tag, id=len(self.ops)))

    def dma(self, stream, fn, r=(), w=(), tag=""):
        self.ops.append(dict(s=stream, fn=fn, r=tuple(r), w=tuple(w),
                             dma=True, tag=tag, id=len(self.ops)))

    def barrier(self):
        n = len(self.ops)
        toks = [("__bar__", n, s) for s in self.STREAMS]
        for s, t in zip(self.STREAMS, toks):
            self.op(s, None, w=[t], tag="barw")
        for s in self.STREAMS:
            self.op(s, None, r=toks, tag="barr")

    NSLOT = 8

    def _analyze(self):
        slot_ctr = {}
        for o in self.ops:
            if o["dma"]:
                q = o["s"]
                n = slot_ctr.get(q, 0)
                slot_ctr[q] = n + 1
                o["sk"] = ("d", q, n % self.NSLOT)
                o["slot_idx"] = n // self.NSLOT + 1
            else:
                o["sk"] = ("c", o["s"])
        last_w = {}
        readers = {}
        deps = [set() for _ in self.ops]
        last_by_sem = {}
        for o in self.ops:
            i = o["id"]
            if o["fn"] is None and not o["dma"]:
                for k, j in last_by_sem.items():
                    if k[1] == o["s"]:
                        deps[i].add(j)
            if o["fn"] is not None:
                last_by_sem[o["sk"]] = i
            for b in o["r"]:
                if b in last_w:
                    deps[i].add(last_w[b])
            for b in o["w"]:
                if b in last_w:
                    deps[i].add(last_w[b])
                for rd in readers.get(b, ()):
                    deps[i].add(rd)
            for b in o["r"]:
                readers.setdefault(b, []).append(i)
            for b in o["w"]:
                last_w[b] = i
                readers[b] = []
        need_inc = set()
        for dd in deps:
            need_inc.update(dd)
        for o in self.ops:
            if o["dma"]:
                need_inc.add(o["id"])
        seq = {}
        for o in self.ops:
            if o["dma"]:
                o["inc_val"] = o["slot_idx"]
            elif o["id"] in need_inc:
                k = o["sk"]
                seq[k] = seq.get(k, 0) + 1
                o["inc_val"] = seq[k]
            else:
                o["inc_val"] = None
        seen = {}
        for o in self.ops:
            need = {}
            for j in deps[o["id"]]:
                k = self.ops[j]["sk"]
                need[k] = max(need.get(k, 0), self.ops[j]["inc_val"])
            waits = []
            for k, v in need.items():
                key = (o["sk"], k)
                if seen.get(key, 0) < v:
                    seen[key] = v
                    waits.append((k, v))
            o["waits"] = waits

    def emit(self, nc):
        self._analyze()
        self._sems = {}
        for st in self.STREAMS:
            cm = nc.semaphore(f"sem_c_{st}")
            self._sems[("c", st)] = cm.__enter__()
            for sl in range(self.NSLOT):
                cm = nc.semaphore(f"sem_d_{st}{sl}")
                self._sems[("d", st, sl)] = cm.__enter__()
        blk = nc.Block()
        block = blk.__enter__()
        sections = dict(pe=block.tensor, dve=block.vector, act=block.scalar,
                        pool=block.gpsimd, sp=block.sync)

        def mk(sname):
            def body(eng):
                for o in self.ops:
                    if o["s"] != sname:
                        continue
                    for (k, v) in o["waits"]:
                        eng.wait_ge(self._sems[k], v * (16 if k[0] == "d" else 1))
                    if o["dma"] and o["slot_idx"] > 1:
                        eng.wait_ge(self._sems[o["sk"]], (o["slot_idx"] - 1) * 16)
                    ins = o["fn"](eng) if o["fn"] is not None else None
                    if o["inc_val"] is not None:
                        tgt = self._sems[o["sk"]]
                        if ins is not None:
                            ins.then_inc(tgt, 16 if o["dma"] else 1)
                        else:
                            eng.nop().then_inc(tgt, 1)
            return body
        for sname in self.STREAMS:
            sections[sname](mk(sname))
        blk.__exit__(None, None, None)


class Ctx:

    def __init__(self, nc, ntok):
        self.nc = nc
        self.ntok = ntok
        self.sch = Sched()
        self._uid = 0
        self.onesrow = nc.alloc_sbuf_tensor("onesrow", [1, GRP], F32)
        self.sumcol = nc.alloc_sbuf_tensor("sumcol", [128, 64], F32)
        self.sumsqcol = nc.alloc_sbuf_tensor("sumsqcol", [128, 64], F32)
        self.ps = [nc.alloc_psum_tensor(f"psb{i}", [128, 512], F32) for i in range(8)]
        self.sch.op("dve", lambda e: e.memset(self.onesrow[:], 1.0),
                    w=[("ones",)], tag="ones")

    def name(self, p):
        self._uid += 1
        return f"{p}_{self._uid}"

    def sb(self, p, shape, dtype):
        if getattr(self, "stack", None) is not None:
            return self.stack.enter_context(
                self.nc.sbuf_tensor(self.name(p), shape, dtype))
        return self.nc.alloc_sbuf_tensor(self.name(p), shape, dtype)

    @contextmanager
    def phase(self):
        prev = getattr(self, "stack", None)
        self.stack = ExitStack()
        try:
            yield
        finally:
            self.stack.close()
            self.stack = prev
            self.sch.barrier()


def dram_tile_keys(tname, row0, rows):
    t0 = row0 // 128
    t1 = (row0 + rows - 1) // 128
    return [("dram", tname, t) for t in range(t0, t1 + 1)]


def build_ffn_phase(cx, w0s, w1s, b0_dram, b1_dram, src_dram, src_name,
                    out_dram, out_name, ntok_all, tag):
    nc, sch = cx.nc, cx.sch
    ntile = cdiv(ntok_all, 128)
    ngrp = cdiv(ntok_all, GRP)
    T = tag

    b0row_sb = cx.sb(T + "b0r", [1, DH], F32)
    b1row_sb = cx.sb(T + "b1r", [1, D], F32)
    sch.dma("sp", lambda e: e.dma_start(out=b0row_sb[:], in_=b0_dram), w=[("b0", T)],
            tag=T + "ldb0")
    sch.dma("sp", lambda e: e.dma_start(out=b1row_sb[:], in_=b1_dram), w=[("b1", T)],
            tag=T + "ldb1")
    mu = cx.sb(T + "mu", [128, ntile], F32)
    var = cx.sb(T + "var", [128, ntile], F32)
    rstd = cx.sb(T + "rstd", [128, ntile], F32)
    xf = cx.sb(T + "xf", [128, 16 * GRP], BF16)
    Hb = cx.sb(T + "H", [128, 64 * GRP], BF16)
    xtok = [cx.sb(T + "xt", [128, D], F32) for _ in range(2)]
    statx = xtok
    xhat = [cx.sb(T + "xh", [128, D], BF16) for _ in range(2)]
    w0b = [cx.sb(T + "w0", [128, 16 * 128], BF16) for _ in range(3)]
    w1b = [cx.sb(T + "w1", [128, 2048], BF16) for _ in range(2)]
    ot = [cx.sb(T + "ot", [128, D], F32) for _ in range(4)]
    ps = cx.ps

    sch.op("dve", lambda e: e.memset(var[:], 1.0), w=[(T, "var", "init")], tag=f"{T}vini")
    for i in range(2):
        sch.op("dve", (lambda i=i: lambda e: e.memset(xhat[i][:], 0.0))(),
               w=[(T, "xhb", i)], tag=f"{T}xhini{i}")
    for t in range(ntile):
        rows = min(128, ntok_all - t * 128)
        xt = statx[t % 2]
        sch.dma("pool", (lambda xt=xt, t=t, rows=rows: lambda e:
                       e.dma_start(out=xt[0:rows, :], in_=src_dram[t * 128:t * 128 + rows, :]))(),
                r=dram_tile_keys(src_name, t * 128, rows), w=[(T, "xtk", t % 2)],
                tag=f"{T}lds{t}")
        sch.op("dve", (lambda xt=xt, t=t, rows=rows: lambda e:
                       e.reduce_sum(cx.sumcol[0:rows, t % 64:t % 64 + 1], xt[0:rows, :], axis=AX.X))(),
               r=[(T, "xtk", t % 2)], w=[(T, "sum", t)], tag=f"{T}sum{t}")
        sch.op("act", (lambda xt=xt, t=t, rows=rows: lambda e:
                       e.activation(xt[0:rows, :], xt[0:rows, :], AF.Square,
                                    accum_out=cx.sumsqcol[0:rows, t % 64:t % 64 + 1]))(),
               r=[(T, "xtk", t % 2), (T, "sum", t)], w=[(T, "xtk", t % 2), (T, "ssq", t)],
               tag=f"{T}ssq{t}")
        sch.op("dve", (lambda t=t, rows=rows: lambda e:
                       e.tensor_scalar(mu[0:rows, t:t + 1], cx.sumcol[0:rows, t % 64:t % 64 + 1],
                                       1.0 / D, None, OP.mult))(),
               r=[(T, "sum", t)], w=[(T, "mu", t)], tag=f"{T}mu{t}")
        sch.op("dve", (lambda t=t, rows=rows: lambda e:
                       e.tensor_tensor(var[0:rows, t:t + 1], mu[0:rows, t:t + 1],
                                       mu[0:rows, t:t + 1], OP.mult))(),
               r=[(T, "mu", t)], w=[(T, "m2", t)], tag=f"{T}m2_{t}")
        sch.op("dve", (lambda t=t, rows=rows: lambda e:
                       e.scalar_tensor_tensor(var[0:rows, t:t + 1],
                                              cx.sumsqcol[0:rows, t % 64:t % 64 + 1], 1.0 / D,
                                              var[0:rows, t:t + 1], OP.mult, OP.subtract))(),
               r=[(T, "ssq", t), (T, "m2", t)], w=[(T, "varp", t)], tag=f"{T}var{t}")
        sch.op("dve", (lambda t=t, rows=rows: lambda e:
                       e.tensor_scalar(var[0:rows, t:t + 1], var[0:rows, t:t + 1],
                                       EPS, None, OP.add))(),
               r=[(T, "varp", t), (T, "var", "init")], w=[(T, "var", t)], tag=f"{T}vep{t}")
    sch.op("act", lambda e: e.activation(rstd[:, 0:ntile], var[:, 0:ntile], AF.Sqrt),
           r=[(T, "var", t) for t in range(ntile)], w=[(T, "sd")], tag=f"{T}sqrt")
    sch.op("dve", lambda e: e.reciprocal(rstd[:, 0:ntile], rstd[:, 0:ntile]),
           r=[(T, "sd")], w=[(T, "rstd")], tag=f"{T}rinv")

    for g in range(ngrp):
        t0g = g * GRP
        gtok = min(GRP, ntok_all - t0g)
        gtile = cdiv(gtok, 128)
        for tt in range(gtile):
            t = (t0g // 128) + tt
            rows = min(128, gtok - tt * 128)
            xt = xtok[t % 2]
            xh = xhat[t % 2]
            sch.dma("sp", (lambda xt=xt, t=t, rows=rows: lambda e:
                           e.dma_start(out=xt[0:rows, :],
                                       in_=src_dram[t * 128:t * 128 + rows, :]))(),
                    r=dram_tile_keys(src_name, t * 128, rows), w=[(T, "xtk", t % 2)],
                    tag=f"{T}ldx{t}")
            sch.op("dve", (lambda xt=xt, xh=xh, t=t, rows=rows: lambda e:
                           e.tensor_scalar(xh[0:rows, :], xt[0:rows, :], mu[0:rows, t:t + 1],
                                           rstd[0:rows, t:t + 1], OP.subtract, OP.mult))(),
                   r=[(T, "xtk", t % 2), (T, "mu", t), (T, "rstd")],
                   w=[(T, "xhb", t % 2)], tag=f"{T}nrm{t}")
            trows = rows if rows % 16 == 0 else min(128, (rows // 16 + 1) * 16)
            sch.dma("act", (lambda xh=xh, tt=tt, trows=trows: lambda e:
                            e.dma_start_transpose(
                                out=xf[:].rearrange("p (k t) -> p k t", k=16)
                                    [:, :, tt * 128:tt * 128 + trows],
                                in_=xh[0:trows, :].rearrange("t (k p) -> t k p", k=16)))(),
                    r=[(T, "xhb", t % 2)], w=[(T, "xf", g, tt)], tag=f"{T}tr{t}")
        xf_keys = [(T, "xf", g, tt) for tt in range(gtile)]

        for m in range(64):
            wb = w0b[m % 3]
            sch.dma("sp", (lambda m=m, wb=wb: lambda e:
                           e.dma_start(out=wb[:],
                                       in_=w0s[m].rearrange("p k j -> p (k j)")))(),
                    r=[("dram", "w0bs", m)], w=[(T, "w0b", m % 3)],
                    tag=f"{T}lw0_{g}_{m}")
            bank = m % 8

            def mm1(e, m=m, wb=wb, bank=bank, gtok=gtok):
                i = e.matmul(ps[bank][:, 0:gtok], b0row_sb[0:1, m * 128:(m + 1) * 128],
                             cx.onesrow[0:1, 0:gtok], start=True, stop=False)
                for kc in range(16):
                    i = e.matmul(ps[bank][:, 0:gtok], wb[:, kc * 128:(kc + 1) * 128],
                                 xf[:, kc * GRP:kc * GRP + gtok],
                                 start=False, stop=(kc == 15))
                return i
            sch.op("pe", mm1, r=[(T, "w0b", m % 3), ("b0", T), ("ones",)] + xf_keys,
                   w=[("psum", bank)], tag=f"{T}mm1_{g}_{m}")
            sch.op("act", (lambda m=m, bank=bank, gtok=gtok: lambda e:
                           e.activation(Hb[:, m * GRP:m * GRP + gtok],
                                        ps[bank][:, 0:gtok], AF.Gelu))(),
                   r=[("psum", bank)], w=[(T, "H", g, m)],
                   tag=f"{T}gel_{g}_{m}")
        H_keys = [(T, "H", g, m) for m in range(64)]

        for rnd in range(4):
            f0 = rnd * 512
            banks = [4 * (rnd % 2) + tt for tt in range(gtile)]
            for kc2 in range(64):
                wb = w1b[(kc2 // 4) % 2]
                if kc2 % 4 == 0:
                    sch.dma("act", (lambda kc2=kc2, f0=f0, wb=wb: lambda e:
                                    e.dma_start(out=wb[:].rearrange("p (q f) -> p q f", q=4),
                                                in_=w1s[kc2:kc2 + 4]
                                                .rearrange("q p f j -> p q (f j)")
                                                [:, :, f0:f0 + 512]))(),
                            r=[("dram", "w1bs", kc2 + qq) for qq in range(4)],
                            w=[(T, "w1b", (kc2 // 4) % 2)],
                            tag=f"{T}lw1_{g}_{rnd}_{kc2}")

                def mm2(e, kc2=kc2, f0=f0, wb=wb, gtile=gtile, gtok=gtok, banks=banks):
                    i = None
                    woff = (kc2 % 4) * 512
                    for tt in range(gtile):
                        rows = min(128, gtok - tt * 128)
                        bank = banks[tt]
                        if kc2 == 0:
                            i = e.matmul(ps[bank][0:rows, 0:512], cx.onesrow[0:1, 0:rows],
                                         b1row_sb[0:1, f0:f0 + 512], start=True, stop=False)
                        i = e.matmul(ps[bank][0:rows, 0:512],
                                     Hb[:, kc2 * GRP + tt * 128:kc2 * GRP + tt * 128 + rows],
                                     wb[:, woff:woff + 512], start=False, stop=(kc2 == 63))
                    return i
                wkeys = [("psum", b) for b in banks] if kc2 in (0, 63) else []
                sch.op("pe", mm2, r=[(T, "w1b", (kc2 // 4) % 2), ("b1", T), ("ones",)] + H_keys,
                       w=wkeys, tag=f"{T}mm2_{g}_{rnd}_{kc2}")
            for tt in range(gtile):
                rows = min(128, gtok - tt * 128)
                t = (t0g // 128) + tt
                bank = banks[tt]
                otb = ot[tt]
                if rnd == 0:
                    sch.dma("act", (lambda otb=otb, t=t, rows=rows: lambda e:
                                    e.dma_start(out=otb[0:rows, :],
                                                in_=src_dram[t * 128:t * 128 + rows, :]))(),
                            r=dram_tile_keys(src_name, t * 128, rows),
                            w=[(T, "otb", tt, "r")], tag=f"{T}ldr{t}")
                sch.op("dve", (lambda bank=bank, otb=otb, rows=rows, f0=f0: lambda e:
                               e.scalar_tensor_tensor(otb[0:rows, f0:f0 + 512],
                                                      ps[bank][0:rows, 0:512], 0.5,
                                                      otb[0:rows, f0:f0 + 512],
                                                      OP.mult, OP.add))(),
                       r=[("psum", bank), (T, "otb", tt, "r")],
                       w=[(T, "otb", tt, rnd)],
                       tag=f"{T}add_{g}_{rnd}_{tt}")
                if rnd == 3:
                    sch.dma("act", (lambda otb=otb, t=t, rows=rows: lambda e:
                                    e.dma_start(out=out_dram[t * 128:t * 128 + rows, :],
                                                in_=otb[0:rows, :]))(),
                            r=[(T, "otb", tt, rr) for rr in range(4)] + [(T, "otb", tt, "r")],
                            w=dram_tile_keys(out_name, t * 128, rows),
                            tag=f"{T}st{t}")


def prep_w0(w0):
    return np.ascontiguousarray(
        w0.reshape(64, 128, 16, 128).transpose(0, 3, 2, 1))


def prep_w1(w1):
    return np.ascontiguousarray(
        w1.reshape(16, 128, 64, 128).transpose(2, 3, 0, 1))


def ffn_reference(x, W0, b0, W1, b1):
    from scipy.special import erf
    ln = (x - x.mean(-1, keepdims=True)) / np.sqrt(x.var(-1, keepdims=True) + EPS)
    h = ln @ W0.T + b0
    Dh = (0.5 * h * (1.0 + erf(h / np.sqrt(2.0)))).astype(np.float32)
    return x + 0.5 * (Dh @ W1.T + b1)


NCH = 33


def build_p2a(cx, W, x1_dram, x1_name, outs, ntok, tag="a"):
    nc, sch = cx.nc, cx.sch
    T = tag
    ntok_all = ntok + 3
    ntile = cdiv(ntok_all, 128)
    nchunk = ntok // CHK
    ngrp = ntok // GRP
    ps = cx.ps

    mu = cx.sb(T + "mu", [128, ntile], F32)
    var = cx.sb(T + "var", [128, ntile], F32)
    rstd = cx.sb(T + "rstd", [128, ntile], F32)
    xtok = [cx.sb(T + "xt", [128, D], F32) for _ in range(2)]
    xhat = [cx.sb(T + "xh", [128, D], BF16) for _ in range(2)]
    xf = cx.sb(T + "xf", [128, 16 * GRP], BF16)
    xfh = cx.sb(T + "xfh", [128, 16 * 16], BF16)
    wxb = [cx.sb(T + "wx", [128, 16 * 128], BF16) for _ in range(3)]
    wzb = [cx.sb(T + "wz", [128, 4 * 512], BF16) for _ in range(2)]
    wdtb = cx.sb(T + "wdt", [128, 16 * 32], BF16)
    rawf = [cx.sb(T + "raw", [128, CHK], F32) for _ in range(2)]
    carry = cx.sb(T + "carry", [128, NCH * 4], F32)
    cvw = cx.sb(T + "cvw", [128, NCH * 4], F32)
    cvb = cx.sb(T + "cvb", [128, NCH], F32)
    cnv = [cx.sb(T + "cnv", [128, CHK], F32) for _ in range(2)]
    xsil = [cx.sb(T + "xsil", [128, NCH * CHK], BF16) for _ in range(2)]
    xtr = cx.sb(T + "xtr", [128, 8192], BF16)
    xst = cx.sb(T + "xst", [128, 8192], BF16)
    xdd = cx.sb(T + "xdd", [128, 8192], BF16)
    btb = cx.sb(T + "bt", [128, 128], BF16)
    zt = [cx.sb(T + "zt", [128, 512], BF16) for _ in range(2)]
    dtrow = cx.sb(T + "dtrow", [1, 64], F32)
    dts = cx.sb(T + "dts", [128, (2 * nchunk + 2) * 32], F32)
    atb = cx.sb(T + "atb", [128, 64], F32)
    acs = cx.sb(T + "acs", [128, nchunk * 64], F32)
    dsb = cx.sb(T + "ds", [128, nchunk * 64], F32)
    alast = cx.sb(T + "alast", [1, nchunk * 32], F32)
    alrow = cx.sb(T + "alrow", [1, 32], F32)
    arow = cx.sb(T + "arow", [1, 32], F32)
    abc = cx.sb(T + "abc", [128, 32], F32)
    albc = cx.sb(T + "albc", [128, 32], F32)
    dcol = cx.sb(T + "dcol", [64, 32], F32)
    rdec = cx.sb(T + "rdec", [1, 32], F32)
    Sst = [cx.sb(T + "S", [64, DI], F32) for _ in range(2)]
    onescolf = cx.sb(T + "onescol", [128, 1], F32)
    tri = cx.sb(T + "tri", [128, 128], F32)
    onesq = cx.sb(T + "onesq", [128, 128], F32)
    xbcbc = cx.sb(T + "xbcbc", [128, NCH], F32)
    zbr = [cx.sb(T + "zbr", [1, 512], F32) for _ in range(2)]
    hmask = cx.sb(T + "hmask", [128, 1], F32)

    for nm, dst, src in [("cvw", cvw, W["cwp"].rearrange("p m t -> p (m t)")),
                         ("cvb", cvb, W["cbp"]), ("tri", tri, W["tri"]),
                         ("onesq", onesq, W["onesq"]), ("xbcbc", xbcbc, W["xbcbc"]),
                         ("hmask", hmask, W["hmask"]),
                         ("arow0", arow, W["alog"])]:
        sch.dma("sp", (lambda dst=dst, src=src: lambda e: e.dma_start(out=dst[:], in_=src))(),
                w=[(T, nm)], tag=T + "ld" + nm)
    sch.dma("pool", lambda e: e.dma_start(out=wdtb[:], in_=W["wipdt"].rearrange("p k j -> p (k j)")),
            w=[(T, "wdt")], tag=T + "ldwdt")
    sch.dma("sp", lambda e: e.dma_start(out=dtrow[0:1, 0:32], in_=W["dtb"]),
            w=[(T, "dtb1")], tag=T + "ldtb1")
    sch.dma("sp", lambda e: e.dma_start(out=dtrow[0:1, 32:64], in_=W["dtbias"]),
            w=[(T, "dtb2")], tag=T + "ldtb2")
    sch.op("dve", lambda e: e.tensor_tensor(dtrow[0:1, 0:32], dtrow[0:1, 0:32],
                                            dtrow[0:1, 32:64], OP.add),
           r=[(T, "dtb1"), (T, "dtb2")], w=[(T, "dtbt")], tag=T + "dtbsum")
    sch.op("dve", lambda e: e.memset(onescolf[:], 1.0), w=[(T, "onec")], tag=T + "onec")
    sch.op("dve", lambda e: e.memset(Sst[0][:], 0.0), w=[(T, "S")], tag=T + "s0")
    sch.op("dve", lambda e: e.memset(rdec[:], 0.0), w=[(T, "rdecr")], tag=T + "rd0")
    sch.op("dve", lambda e: e.memset(var[:], 1.0), w=[(T, "var", "init")], tag=T + "vini")
    for i in range(2):
        sch.op("dve", (lambda i=i: lambda e: e.memset(xhat[i][:], 0.0))(),
               w=[(T, "xhb", i)], tag=T + f"xhini{i}")
    sch.op("act", lambda e: e.activation(arow[:], arow[:], AF.Exp),
           r=[(T, "arow0")], w=[(T, "arow1")], tag=T + "aexp")
    sch.op("dve", lambda e: e.tensor_scalar(arow[:], arow[:], -1.0, None, OP.mult),
           r=[(T, "arow1")], w=[(T, "arow")], tag=T + "aneg")
    sch.op("pe", lambda e: e.matmul(ps[5][0:128, 0:32], cx.onesrow[0:1, 0:128], arow[:],
                                    start=True, stop=True),
           r=[(T, "arow"), ("ones",)], w=[("psum", 5)], tag=T + "abcm")
    sch.op("dve", lambda e: e.tensor_copy(abc[:], ps[5][0:128, 0:32]),
           r=[("psum", 5)], w=[(T, "abc")], tag=T + "abcc")

    for t in range(ntile):
        rows = min(128, ntok_all - t * 128)
        xt = xtok[t % 2]
        sch.dma("pool", (lambda xt=xt, t=t, rows=rows: lambda e:
                       e.dma_start(out=xt[0:rows, :], in_=x1_dram[t * 128:t * 128 + rows, :]))(),
                r=dram_tile_keys(x1_name, t * 128, rows), w=[(T, "xtk", t % 2)], tag=f"{T}lds{t}")
        sch.op("dve", (lambda xt=xt, t=t, rows=rows: lambda e:
                       e.reduce_sum(cx.sumcol[0:rows, t % 64:t % 64 + 1], xt[0:rows, :], axis=AX.X))(),
               r=[(T, "xtk", t % 2)], w=[(T, "sum", t)], tag=f"{T}sum{t}")
        sch.op("act", (lambda xt=xt, t=t, rows=rows: lambda e:
                       e.activation(xt[0:rows, :], xt[0:rows, :], AF.Square,
                                    accum_out=cx.sumsqcol[0:rows, t % 64:t % 64 + 1]))(),
               r=[(T, "xtk", t % 2), (T, "sum", t)], w=[(T, "xtk", t % 2), (T, "ssq", t)],
               tag=f"{T}ssq{t}")
        sch.op("dve", (lambda t=t, rows=rows: lambda e:
                       e.tensor_scalar(mu[0:rows, t:t + 1], cx.sumcol[0:rows, t % 64:t % 64 + 1],
                                       1.0 / D, None, OP.mult))(),
               r=[(T, "sum", t)], w=[(T, "mu", t)], tag=f"{T}mu{t}")
        sch.op("dve", (lambda t=t, rows=rows: lambda e:
                       e.tensor_tensor(var[0:rows, t:t + 1], mu[0:rows, t:t + 1],
                                       mu[0:rows, t:t + 1], OP.mult))(),
               r=[(T, "mu", t)], w=[(T, "m2", t)], tag=f"{T}m2_{t}")
        sch.op("dve", (lambda t=t, rows=rows: lambda e:
                       e.scalar_tensor_tensor(var[0:rows, t:t + 1],
                                              cx.sumsqcol[0:rows, t % 64:t % 64 + 1], 1.0 / D,
                                              var[0:rows, t:t + 1], OP.mult, OP.subtract))(),
               r=[(T, "ssq", t), (T, "m2", t)], w=[(T, "varp", t)], tag=f"{T}var{t}")
        sch.op("dve", (lambda t=t, rows=rows: lambda e:
                       e.tensor_scalar(var[0:rows, t:t + 1], var[0:rows, t:t + 1],
                                       EPS, None, OP.add))(),
               r=[(T, "varp", t), (T, "var", "init")], w=[(T, "var", t)], tag=f"{T}vep{t}")
    sch.op("act", lambda e: e.activation(rstd[:, 0:ntile], var[:, 0:ntile], AF.Sqrt),
           r=[(T, "var", t) for t in range(ntile)], w=[(T, "sd")], tag=T + "sqrt")
    sch.op("dve", lambda e: e.reciprocal(rstd[:, 0:ntile], rstd[:, 0:ntile]),
           r=[(T, "sd")], w=[(T, "rstd")], tag=T + "rinv")

    def emit_ln_tile(t, xf_dst, dst_col, dkeys, grpw):
        rows = min(128, ntok_all - t * 128)
        xt = xtok[t % 2]
        xh = xhat[t % 2]
        sch.dma("sp", (lambda xt=xt, t=t, rows=rows: lambda e:
                       e.dma_start(out=xt[0:rows, :], in_=x1_dram[t * 128:t * 128 + rows, :]))(),
                r=dram_tile_keys(x1_name, t * 128, rows), w=[(T, "xtk", t % 2)], tag=f"{T}ldx{t}")
        sch.op("dve", (lambda xt=xt, xh=xh, t=t, rows=rows: lambda e:
                       e.tensor_scalar(xh[0:rows, :], xt[0:rows, :], mu[0:rows, t:t + 1],
                                       rstd[0:rows, t:t + 1], OP.subtract, OP.mult))(),
               r=[(T, "xtk", t % 2), (T, "mu", t), (T, "rstd")], w=[(T, "xhb", t % 2)],
               tag=f"{T}nrm{t}")
        trows = rows if rows % 16 == 0 else min(128, (rows // 16 + 1) * 16)
        sch.dma("act", (lambda xh=xh, trows=trows, xf_dst=xf_dst, dst_col=dst_col, grpw=grpw:
                        lambda e: e.dma_start_transpose(
                            out=xf_dst[:].rearrange("p (k t) -> p k t", k=16)
                                [:, :, dst_col:dst_col + trows],
                            in_=xh[0:trows, :].rearrange("t (k p) -> t k p", k=16)))(),
                r=[(T, "xhb", t % 2)], w=dkeys, tag=f"{T}tr{t}")

    halo_t = ntok // 128
    emit_ln_tile(halo_t, xfh, 0, [(T, "xfh")], 16)
    for m in range(NCH):
        wb = wxb[m % 3]
        sch.dma("sp", (lambda m=m, wb=wb: lambda e:
                       e.dma_start(out=wb[:], in_=W["wipx"][m].rearrange("p k j -> p (k j)")))(),
                r=[("dram", "wipxs", m)], w=[(T, "wxb", m % 3)], tag=f"{T}hlw{m}")

        def mmh(e, m=m, wb=wb):
            i = None
            for kc in range(16):
                i = e.matmul(ps[6][0:128, 0:3], wb[:, kc * 128:(kc + 1) * 128],
                             xfh[:, kc * 16:kc * 16 + 3], start=(kc == 0), stop=(kc == 15))
            return i
        sch.op("pe", mmh, r=[(T, "wxb", m % 3), (T, "xfh"), ("ones",)],
               w=[("psum", 6)], tag=f"{T}hmm{m}")
        sch.op("dve", (lambda m=m: lambda e:
                       e.tensor_scalar(carry[:, m * 4:m * 4 + 3], ps[6][0:128, 0:3],
                                       xbcbc[:, m:m + 1], hmask[:, 0:1],
                                       OP.add, OP.mult))(),
               r=[("psum", 6), (T, "hmask"), (T, "xbcbc")], w=[(T, "carry", m)],
               tag=f"{T}hcar{m}")

    for g in range(ngrp):
        for tt in range(GRP // 128):
            t = g * (GRP // 128) + tt
            emit_ln_tile(t, xf, tt * 128, [(T, "xf", g, tt)], GRP)
        xf_keys = [(T, "xf", g, tt) for tt in range(GRP // 128)]

        for zb in range(8):
            zr = zbr[zb % 2]
            sch.dma("sp", (lambda zb=zb, zr=zr: lambda e:
                           e.dma_start(out=zr[:], in_=W["zb"][zb:zb + 1, :]))(),
                    r=[("wipz",)], w=[(T, "zbr", zb % 2)], tag=f"{T}lzb{g}_{zb}")
            for kcq in range(4):
                wb = wzb[kcq % 2]
                sch.dma("act", (lambda zb=zb, kcq=kcq, wb=wb: lambda e:
                               e.dma_start(out=wb[:],
                                           in_=W["wipz"][zb, kcq]))(),
                        r=[("dram", "wipzs", zb, kcq)], w=[(T, "wzb", kcq % 2)],
                        tag=f"{T}lwz{g}_{zb}_{kcq}")

                def mmz(e, zb=zb, kcq=kcq, wb=wb, g=g, zr=zr):
                    i = None
                    for tt in range(4):
                        bank = 3 + tt
                        if kcq == 0:
                            i = e.matmul(ps[bank][0:128, 0:512], cx.onesrow[0:1, 0:128],
                                         zr[0:1, 0:512],
                                         start=True, stop=False)
                        for kk in range(4):
                            kc = kcq * 4 + kk
                            i = e.matmul(ps[bank][0:128, 0:512],
                                         xf[:, kc * GRP + tt * 128:kc * GRP + tt * 128 + 128],
                                         wb[:, kk * 512:(kk + 1) * 512],
                                         start=False, stop=(kc == 15))
                    return i
                wk = [("psum", 3 + tt) for tt in range(4)] if kcq in (0, 3) else []
                sch.op("pe", mmz, r=[(T, "wzb", kcq % 2), (T, "zbr", zb % 2), ("ones",)] + xf_keys,
                       w=wk, tag=f"{T}mmz{g}_{zb}_{kcq}")
            for tt in range(4):
                ztb = zt[tt % 2]
                sch.op("act", (lambda ztb=ztb, tt=tt: lambda e:
                               e.copy(ztb[:], ps[3 + tt][0:128, 0:512]))(),
                       r=[("psum", 3 + tt)], w=[(T, "ztb", tt % 2)], tag=f"{T}zev{g}_{zb}_{tt}")
                sch.dma("act", (lambda ztb=ztb, g=g, tt=tt, zb=zb: lambda e:
                                e.dma_start(out=outs["z"][g * GRP + tt * 128:
                                                          g * GRP + tt * 128 + 128,
                                                          zb * 512:(zb + 1) * 512],
                                            in_=ztb[:]))(),
                        r=[(T, "ztb", tt % 2)], w=[("dram", "z", g, tt, zb)],
                        tag=f"{T}sz{g}_{tt}_{zb}")

        for c in range(2 * g, 2 * g + 2):
            coff = (c % 2) * CHK
            xs_c = xsil[c % 2]
            for m in range(NCH):
                wb = wxb[m % 3]
                sch.dma("sp", (lambda m=m, wb=wb: lambda e:
                               e.dma_start(out=wb[:],
                                           in_=W["wipx"][m].rearrange("p k j -> p (k j)")))(),
                        r=[("dram", "wipxs", m)], w=[(T, "wxb", m % 3)], tag=f"{T}lwx{c}_{m}")
                bank = m % 2

                def mmx(e, m=m, wb=wb, bank=bank, coff=coff):
                    i = None
                    for kc in range(16):
                        i = e.matmul(ps[bank][0:128, 0:CHK], wb[:, kc * 128:(kc + 1) * 128],
                                     xf[:, kc * GRP + coff:kc * GRP + coff + CHK],
                                     start=(kc == 0), stop=(kc == 15))
                    return i
                sch.op("pe", mmx, r=[(T, "wxb", m % 3), ("ones",)] + xf_keys,
                       w=[("psum", bank)], tag=f"{T}mmx{c}_{m}")
                raw = rawf[m % 2]
                sch.op("dve", (lambda raw=raw, bank=bank, m=m: lambda e:
                               e.tensor_scalar(raw[:], ps[bank][0:128, 0:CHK],
                                               xbcbc[:, m:m + 1], None, OP.add))(),
                       r=[("psum", bank), (T, "xbcbc")], w=[(T, "raw", m % 2)],
                       tag=f"{T}rev{c}_{m}")
                cv = cnv[c % 2]
                sch.op("dve", (lambda raw=raw, cv=cv, m=m: lambda e:
                               e.tensor_scalar(cv[:], raw[:], cvw[:, m * 4 + 3:m * 4 + 4],
                                               None, OP.mult))(),
                       r=[(T, "raw", m % 2), (T, "cvw")], w=[(T, "cv", c % 2)],
                       tag=f"{T}cv3_{c}_{m}")
                for tap in range(3):
                    sh = 3 - tap
                    sch.op("dve", (lambda raw=raw, cv=cv, m=m, tap=tap, sh=sh: lambda e:
                                   e.scalar_tensor_tensor(cv[:, sh:CHK], raw[:, 0:CHK - sh],
                                                          cvw[:, m * 4 + tap:m * 4 + tap + 1],
                                                          cv[:, sh:CHK], OP.mult, OP.add))(),
                           r=[(T, "raw", m % 2), (T, "cvw"), (T, "cv", c % 2)],
                           w=[(T, "cv", c % 2)], tag=f"{T}cvt{c}_{m}_{tap}")
                    sch.op("dve", (lambda cv=cv, m=m, tap=tap, sh=sh: lambda e:
                                   e.scalar_tensor_tensor(cv[:, 0:sh],
                                                          carry[:, m * 4 + tap:m * 4 + 3],
                                                          cvw[:, m * 4 + tap:m * 4 + tap + 1],
                                                          cv[:, 0:sh], OP.mult, OP.add))(),
                           r=[(T, "carry", m), (T, "cvw"), (T, "cv", c % 2)],
                           w=[(T, "cv", c % 2)], tag=f"{T}cvc{c}_{m}_{tap}")
                sch.op("dve", (lambda raw=raw, m=m: lambda e:
                               e.tensor_copy(carry[:, m * 4:m * 4 + 3], raw[:, CHK - 3:CHK]))(),
                       r=[(T, "raw", m % 2), (T, "cv", c % 2)], w=[(T, "carry", m)],
                       tag=f"{T}cup{c}_{m}")
                sch.op("act", (lambda cv=cv, m=m, xs_c=xs_c: lambda e:
                               e.activation(xs_c[:, m * CHK:(m + 1) * CHK], cv[:],
                                            AF.Silu, bias=cvb[:, m:m + 1]))(),
                       r=[(T, "cv", c % 2), (T, "cvb")], w=[(T, "xsil", c % 2, m)],
                       tag=f"{T}sil{c}_{m}")
            sch.dma("act", (lambda c=c, xs_c=xs_c: lambda e:
                            e.dma_start(out=outs["bc"][:, c * CHK:(c + 1) * CHK],
                                        in_=xs_c[:, 32 * CHK:33 * CHK]))(),
                    r=[(T, "xsil", c % 2, 32)], w=[("dram", "bc", c)], tag=f"{T}sbc{c}")

            for sig in range(2):
                t2 = c * 2 + sig
                tok0 = coff + sig * 128

                def mmd(e, tok0=tok0):
                    i = e.matmul(ps[5][0:128, 64:96], cx.onesrow[0:1, 0:128],
                                 dtrow[0:1, 0:32], start=True, stop=False)
                    for kc in range(16):
                        i = e.matmul(ps[5][0:128, 64:96],
                                     xf[:, kc * GRP + tok0:kc * GRP + tok0 + 128],
                                     wdtb[:, kc * 32:(kc + 1) * 32], start=False,
                                     stop=(kc == 15))
                    return i
                sch.op("pe", mmd, r=[(T, "wdt"), (T, "dtbt"), ("ones",)] + xf_keys,
                       w=[("psum", 5)], tag=f"{T}mmd{t2}")
                dv = dts[:, t2 * 32:(t2 + 1) * 32]
                sch.op("act", (lambda dv=dv: lambda e:
                               e.activation(dv, ps[5][0:128, 64:96], AF.Abs))(),
                       r=[("psum", 5)], w=[(T, "dt1", t2)], tag=f"{T}dta{t2}")
                sch.op("act", (lambda dv=dv: lambda e:
                               e.activation(dv, dv, AF.Exp, scale=-1.0))(),
                       r=[(T, "dt1", t2)], w=[(T, "dt2", t2)], tag=f"{T}dte{t2}")
                sch.op("act", (lambda dv=dv: lambda e:
                               e.activation(dv, dv, AF.Ln, bias=1.0))(),
                       r=[(T, "dt2", t2)], w=[(T, "dt3", t2)], tag=f"{T}dtl{t2}")
                rl = dts[:, (2 * nchunk + sig) * 32:(2 * nchunk + sig + 1) * 32]
                sch.op("act", (lambda rl=rl: lambda e:
                               e.activation(rl, ps[5][0:128, 64:96], AF.Relu))(),
                       r=[("psum", 5), (T, "dt3", t2)], w=[(T, "dt4", t2), ("psum", 5)],
                       tag=f"{T}dtr{t2}")
                sch.op("dve", (lambda dv=dv, rl=rl: lambda e:
                               e.tensor_tensor(dv, dv, rl, OP.add))(),
                       r=[(T, "dt3", t2), (T, "dt4", t2)], w=[(T, "dts", t2)],
                       tag=f"{T}dtf{t2}")
                sch.op("dve", (lambda dv=dv, sig=sig: lambda e:
                               e.tensor_tensor(atb[:, sig * 32:(sig + 1) * 32], dv,
                                               abc[:], OP.mult))(),
                       r=[(T, "dts", t2), (T, "abc")], w=[(T, "at", c, sig)],
                       tag=f"{T}at{t2}")

            def mmacs0(e, c=c):
                return e.matmul(ps[5][0:128, 96:128], tri[:], atb[:, 0:32],
                                start=True, stop=True)
            sch.op("pe", mmacs0, r=[(T, "at", c, 0), (T, "tri")], w=[("psum", 5)],
                   tag=f"{T}acs0_{c}")
            sch.op("dve", (lambda c=c: lambda e:
                           e.tensor_copy(acs[:, c * 64:c * 64 + 32], ps[5][0:128, 96:128]))(),
                   r=[("psum", 5)], w=[(T, "acs", c, 0)], tag=f"{T}acsc0_{c}")

            def mmacs1(e, c=c):
                i = e.matmul(ps[5][0:128, 96:128], onesq[:], atb[:, 0:32],
                             start=True, stop=False)
                return e.matmul(ps[5][0:128, 96:128], tri[:], atb[:, 32:64],
                                start=False, stop=True)
            sch.op("pe", mmacs1, r=[(T, "at", c, 0), (T, "at", c, 1), (T, "tri"),
                                    (T, "onesq"), (T, "acs", c, 0)],
                   w=[("psum", 5)], tag=f"{T}acs1_{c}")
            sch.op("dve", (lambda c=c: lambda e:
                           e.tensor_copy(acs[:, c * 64 + 32:c * 64 + 64], ps[5][0:128, 96:128]))(),
                   r=[("psum", 5)], w=[(T, "acs", c, 1)], tag=f"{T}acsc1_{c}")

            def mmal(e, c=c):
                i = e.matmul(ps[5][0:1, 32:64], onescolf[:], atb[:, 0:32],
                             start=True, stop=False)
                return e.matmul(ps[5][0:1, 32:64], onescolf[:], atb[:, 32:64],
                                start=False, stop=True)
            sch.op("pe", mmal, r=[(T, "at", c, 0), (T, "at", c, 1), (T, "onec"),
                                  (T, "acs", c, 1)],
                   w=[("psum", 5)], tag=f"{T}al{c}")
            sch.op("dve", (lambda c=c: lambda e:
                           e.tensor_copy(alast[0:1, c * 32:(c + 1) * 32], ps[5][0:1, 32:64]))(),
                   r=[("psum", 5)], w=[(T, "alast", c)], tag=f"{T}alc{c}")
            sch.op("dve", (lambda c=c: lambda e:
                           e.tensor_tensor(rdec[:], rdec[:], alast[0:1, c * 32:(c + 1) * 32],
                                           OP.add))(),
                   r=[(T, "alast", c), (T, "rdecr")], w=[(T, "rdecr")], tag=f"{T}rda{c}")
            sch.op("pe", (lambda c=c: lambda e:
                          e.matmul(ps[5][0:128, 96:128], cx.onesrow[0:1, 0:128],
                                   alast[0:1, c * 32:(c + 1) * 32], start=True, stop=True))(),
                   r=[(T, "alast", c), ("ones",), (T, "acs", c, 1)], w=[("psum", 5)],
                   tag=f"{T}albm{c}")
            sch.op("dve", (lambda c=c: lambda e:
                           e.tensor_copy(albc[:], ps[5][0:128, 96:128]))(),
                   r=[("psum", 5)], w=[(T, "albc", c)], tag=f"{T}albc{c}")
            for sig in range(2):
                dss = dsb[:, c * 64 + sig * 32:c * 64 + (sig + 1) * 32]
                sch.op("dve", (lambda dss=dss, c=c, sig=sig: lambda e:
                               e.tensor_tensor(dss, albc[:],
                                               acs[:, c * 64 + sig * 32:c * 64 + (sig + 1) * 32],
                                               OP.subtract))(),
                       r=[(T, "albc", c), (T, "acs", c, sig)], w=[(T, "dsd", c, sig)],
                       tag=f"{T}dss{c}_{sig}")
                sch.op("act", (lambda dss=dss, c=c, sig=sig: lambda e:
                               e.activation(dss, dss, AF.Exp))(),
                       r=[(T, "dsd", c, sig)], w=[(T, "ds", c, sig)], tag=f"{T}dse{c}_{sig}")
            sch.op("act", (lambda c=c: lambda e:
                           e.activation(alrow[:], alast[0:1, c * 32:(c + 1) * 32], AF.Exp))(),
                   r=[(T, "alast", c)], w=[(T, "alrow", c)], tag=f"{T}ale{c}")
            sch.op("pe", (lambda c=c: lambda e:
                          e.matmul(ps[5][0:64, 32:64], cx.onesrow[0:1, 0:64], alrow[:],
                                   start=True, stop=True))(),
                   r=[(T, "alrow", c), ("ones",), (T, "alast", c)], w=[("psum", 5)],
                   tag=f"{T}dcm{c}")
            sch.op("dve", (lambda c=c: lambda e:
                           e.tensor_copy(dcol[:], ps[5][0:64, 32:64]))(),
                   r=[("psum", 5)], w=[(T, "dcol", c)], tag=f"{T}dcc{c}")

            xkeys = [(T, "xsil", c % 2, m) for m in range(32)]
            sch.dma("act", (lambda xs_c=xs_c: lambda e:
                            e.dma_start_transpose(
                                out=xtr[:].rearrange("s (h p) -> s h p", h=64),
                                in_=xs_c[:, 0:32 * CHK].rearrange("p (h s) -> p h s", h=64)))(),
                    r=xkeys, w=[(T, "xtr")], tag=f"{T}xtr{c}")
            for h in range(NH):
                for sig in range(2):
                    sl = slice((h * 2 + sig) * 128, (h * 2 + sig) * 128 + 128)
                    t2 = c * 2 + sig
                    sch.op("dve", (lambda sl=sl, t2=t2, h=h: lambda e:
                                   e.tensor_scalar(xst[:, sl], xtr[:, sl],
                                                   dts[:, t2 * 32 + h:t2 * 32 + h + 1],
                                                   None, OP.mult))(),
                           r=[(T, "xtr"), (T, "dts", t2)], w=[(T, "xst", h, sig)],
                           tag=f"{T}xs{c}_{h}_{sig}")
                    sch.op("pool", (lambda sl=sl, c=c, sig=sig, h=h: lambda e:
                                    e.tensor_scalar(xdd[:, sl], xst[:, sl],
                                                    dsb[:, c * 64 + sig * 32 + h:
                                                         c * 64 + sig * 32 + h + 1],
                                                    None, OP.mult))(),
                           r=[(T, "xst", h, sig), (T, "ds", c, sig)],
                           w=[(T, "xdd", h, sig)], tag=f"{T}xd{c}_{h}_{sig}")
            sch.dma("act", (lambda c=c: lambda e:
                            e.dma_start(out=outs["xs"][c],
                                        in_=xst[:]))(),
                    r=[(T, "xst", h, sig) for h in range(NH) for sig in range(2)],
                    w=[("dram", "xs", c)], tag=f"{T}sxs{c}")
            sch.dma("act", (lambda xs_c=xs_c: lambda e:
                            e.dma_start_transpose(
                                out=btb[:].rearrange("s (g n) -> s g n", g=2),
                                in_=xs_c[0:64, 32 * CHK:33 * CHK]
                                    .rearrange("n (g s) -> n g s", g=2)))(),
                    r=[(T, "xsil", c % 2, 32)], w=[(T, "btb")], tag=f"{T}btr{c}")
            Sold = Sst[c % 2]
            Snew = Sst[(c + 1) % 2]
            sch.dma("pool", (lambda c=c, Sold=Sold: lambda e:
                             e.dma_start(out=outs["pref"][c], in_=Sold[:]))(),
                    r=[(T, "S")], w=[("dram", "pref", c)], tag=f"{T}spref{c}")
            for h in range(NH):
                bank = 6 + h % 2

                def mmst(e, h=h, bank=bank):
                    i = e.matmul(ps[bank][0:64, 0:128], btb[:, 0:64],
                                 xdd[:, (h * 2) * 128:(h * 2) * 128 + 128],
                                 start=True, stop=False)
                    return e.matmul(ps[bank][0:64, 0:128], btb[:, 64:128],
                                    xdd[:, (h * 2 + 1) * 128:(h * 2 + 1) * 128 + 128],
                                    start=False, stop=True)
                sch.op("pe", mmst, r=[(T, "btb"), (T, "xdd", h, 0), (T, "xdd", h, 1)],
                       w=[("psum", bank)], tag=f"{T}mst{c}_{h}")
                sch.op("dve", (lambda h=h, bank=bank, Sold=Sold, Snew=Snew: lambda e:
                               e.scalar_tensor_tensor(Snew[:, h * 128:(h + 1) * 128],
                                                      Sold[:, h * 128:(h + 1) * 128],
                                                      dcol[:, h:h + 1],
                                                      ps[bank][0:64, 0:128], OP.mult, OP.add))(),
                       r=[("psum", bank), (T, "S"), (T, "dcol", c)],
                       w=[(T, "Snew", c, h), (T, "S")], tag=f"{T}scan{c}_{h}")
            sch.op("dve", lambda e: e.engine_nop(),
                   r=[(T, "Snew", c, h) for h in range(NH)], w=[(T, "S")],
                   tag=f"{T}sflip{c}")

    sch.dma("pool", lambda e: e.dma_start(out=outs["sfin"], in_=Sst[nchunk % 2][:]),
            r=[(T, "S")], w=[("dram", "sfin")], tag=T + "ssf")
    sch.dma("pool", lambda e: e.dma_start(out=outs["rdec"], in_=rdec[:]),
            r=[(T, "rdecr")], w=[("dram", "rdec")], tag=T + "srd")
    sch.dma("pool", lambda e: e.dma_start(out=outs["acs"], in_=acs[:]),
            r=[(T, "acs", c2, s2) for c2 in range(nchunk) for s2 in range(2)],
            w=[("dram", "acs")], tag=T + "sacs")
    sch.dma("pool", lambda e: e.dma_start(out=outs["alast"], in_=alast[:]),
            r=[(T, "alast", c2) for c2 in range(nchunk)], w=[("dram", "alast")], tag=T + "sal")


def build_p2b(cx, W, ins, x1_dram, x1_name, x2_dram, x2_name, ntok, tag="b"):
    nc, sch = cx.nc, cx.sch
    T = tag
    nchunk = ntok // CHK
    ps = cx.ps
    HB = 4

    sel = cx.sb(T + "sel", [32, DI], F32)
    maskA = cx.sb(T + "maskA", [128, 256], F32)
    identf = cx.sb(T + "id", [128, 128], F32)
    lnwc = cx.sb(T + "lnwc", [128, 32], F32)
    lnbc = cx.sb(T + "lnbc", [128, 32], F32)
    lnbcb = cx.sb(T + "lnbcb", [128, 32], BF16)
    opbr = cx.sb(T + "opb", [1, D], F32)
    oBr = cx.sb(T + "oB", [1, D], F32)
    wsf = [cx.sb(T + "wsf", [128, D], F32) for _ in range(1)]
    wsb = [cx.sb(T + "wsb", [128, D], BF16) for _ in range(2)]
    wst = cx.sb(T + "wst", [128, D], BF16)
    acsb = [cx.sb(T + "acs", [128, 64], F32) for _ in range(2)]
    acm = cx.sb(T + "acm", [32, 256], F32)
    alrow = cx.sb(T + "alast", [1, nchunk * 32], F32)
    alpre = cx.sb(T + "alpre", [1, 32], F32)
    edrow = cx.sb(T + "edrow", [1, 32], F32)
    dpcol = cx.sb(T + "dpcol", [64, 32], F32)
    Sin = cx.sb(T + "Sin", [64, DI], F32)
    prefb = cx.sb(T + "pref", [64, DI], F32)
    sgb = cx.sb(T + "sgb", [64, DI], BF16)
    xsb = cx.sb(T + "xsb", [128, 8192], BF16)
    bcbB = [cx.sb(T + "bcbB", [64, CHK], BF16) for _ in range(2)]
    bcbC = [cx.sb(T + "bcbC", [64, CHK], BF16) for _ in range(2)]
    btc0 = cx.sb(T + "btc0", [128, 256], F32)
    btc1 = cx.sb(T + "btc1", [128, 128], F32)
    t1b = cx.sb(T + "t1b", [128, HB * 384], F32)
    exb = cx.sb(T + "exb", [128, HB * 384], BF16)
    gtb = cx.sb(T + "gtb", [128, HB * 384], BF16)
    ebuf = cx.sb(T + "ebuf", [128, 64], F32)
    ybig = [cx.sb(T + "ybig", [128, DI], BF16) for _ in range(2)]
    ztl = cx.sb(T + "ztl", [128, DI], BF16)
    szb = cx.sb(T + "szb", [128, DI], BF16)
    yf = [cx.sb(T + "yf", [128, DI], BF16) for _ in range(2)]
    mu2 = cx.sb(T + "mu2", [128, 2 * nchunk], F32)
    var2 = cx.sb(T + "var2", [128, 2 * nchunk], F32)
    rstd2 = cx.sb(T + "rstd2", [128, 2 * nchunk], F32)
    x2t = [cx.sb(T + "x2t", [128, D], F32) for _ in range(2)]
    onescolf = cx.sb(T + "onescol", [128, 1], F32)

    for nm, dst, src in [("sel", sel, W["sel"]), ("maskA", maskA, W["maskA"]),
                         ("id", identf, W["identf"]), ("lnwc", lnwc, W["lnwc"]),
                         ("lnbc", lnbc, W["lnbc"]), ("opb", opbr, W["opb"]),
                         ("alast", alrow, ins["alast"])]:
        sch.dma("sp", (lambda dst=dst, src=src: lambda e: e.dma_start(out=dst[:], in_=src))(),
                w=[(T, nm)], tag=T + "ld" + nm)
    sch.op("dve", lambda e: e.memset(onescolf[:], 1.0), w=[(T, "onec")], tag=T + "onec")
    sch.op("dve", lambda e: e.memset(alpre[:], 0.0), w=[(T, "alpre")], tag=T + "alp0")
    sch.op("dve", lambda e: e.memset(var2[:], 1.0), w=[(T, "v2i")], tag=T + "v2i")
    sch.op("dve", lambda e: e.tensor_copy(lnbcb[:], lnbc[:]),
           r=[(T, "lnbc")], w=[(T, "lnbcb")], tag=T + "lnbcb")

    for kc in range(32):
        wf = wsf[0]
        wb2 = wsb[kc % 2]
        sch.dma("sp", (lambda kc=kc, wf=wf: lambda e:
                       e.dma_start(out=wf[:], in_=W["wops"][kc]))(),
                r=[("wops",)], w=[(T, "wsf", 0)], tag=f"{T}lwo{kc}")
        sch.op("dve", (lambda kc=kc, wf=wf, wb2=wb2: lambda e:
                       e.tensor_scalar(wb2[:], wf[:], lnwc[:, kc:kc + 1], None, OP.mult))(),
               r=[(T, "wsf", 0), (T, "lnwc")], w=[(T, "wsb", kc % 2)],
               tag=f"{T}wsc{kc}")
        sch.dma("act", (lambda kc=kc, wb2=wb2: lambda e:
                        e.dma_start(out=W["wopsb"][kc], in_=wb2[:]))(),
                r=[(T, "wsb", kc % 2)], w=[("dram", "wopsb", kc)], tag=f"{T}swo{kc}")
        sch.op("pool", (lambda kc=kc, wf=wf: lambda e:
                        e.tensor_copy(wst[:], wf[:]))(),
               r=[(T, "wsf", 0)], w=[(T, "wst")], tag=f"{T}wcp{kc}")

        def mmob(e, kc=kc):
            i = None
            for fb in range(4):
                off = 32 * (fb // 2)
                i = e.matmul(ps[6 + fb % 2][off:off + 1, 0:512],
                             lnbcb[:, kc:kc + 1],
                             wst[:, fb * 512:(fb + 1) * 512],
                             start=(kc == 0), stop=(kc == 31))
            return i
        sch.op("pe", mmob, r=[(T, "wst"), (T, "lnbcb")],
               w=[("psum", 6), ("psum", 7)] if kc in (0, 31) else [], tag=f"{T}mob{kc}")
    for fb in range(4):
        sch.op("dve", (lambda fb=fb: lambda e:
                       e.tensor_tensor(oBr[0:1, fb * 512:(fb + 1) * 512],
                                       ps[6 + fb % 2][32 * (fb // 2):32 * (fb // 2) + 1, 0:512],
                                       opbr[0:1, fb * 512:(fb + 1) * 512], OP.add))(),
               r=[("psum", 6), ("psum", 7), (T, "opb")], w=[(T, "oB", fb)], tag=f"{T}ob{fb}")
    oB_keys = [(T, "oB", fb) for fb in range(4)]

    sch.op("dve", lambda e: e.memset(Sin[:], 0.0), w=[(T, "Sin")], tag=T + "sin0")
    for j in range(3):
        sch.dma("sp", (lambda j=j: lambda e:
                       e.dma_start(out=prefb[:], in_=ins["pstate"][j]))(),
                r=[("pstate",)], w=[(T, "prefb")], tag=f"{T}lps{j}")
        sch.dma("sp", (lambda j=j: lambda e:
                       e.dma_start(out=edrow[:], in_=ins["pdec"][j:j + 1, :]))(),
                r=[("pdec",)], w=[(T, "edrow")], tag=f"{T}lpd{j}")
        sch.op("act", lambda e: e.activation(edrow[:], edrow[:], AF.Exp),
               r=[(T, "edrow")], w=[(T, "edrow")], tag=f"{T}pde{j}")
        sch.op("pe", lambda e: e.matmul(ps[5][0:64, 0:32], cx.onesrow[0:1, 0:64],
                                        edrow[:], start=True, stop=True),
               r=[(T, "edrow"), ("ones",)], w=[("psum", 5)], tag=f"{T}pdm{j}")
        sch.op("dve", lambda e: e.tensor_copy(dpcol[:], ps[5][0:64, 0:32]),
               r=[("psum", 5)], w=[(T, "dpcol")], tag=f"{T}pdc{j}")
        for h in range(NH):
            sch.op("dve", (lambda h=h: lambda e:
                           e.scalar_tensor_tensor(Sin[:, h * 128:(h + 1) * 128],
                                                  Sin[:, h * 128:(h + 1) * 128],
                                                  dpcol[:, h:h + 1],
                                                  prefb[:, h * 128:(h + 1) * 128],
                                                  OP.mult, OP.add))(),
                   r=[(T, "Sin"), (T, "dpcol"), (T, "prefb")], w=[(T, "Sin")],
                   tag=f"{T}sf{j}_{h}")

    for c in range(nchunk):
        lam2 = [c * 2, c * 2 + 1]
        ab = acsb[c % 2]
        sch.dma("sp", (lambda c=c, ab=ab: lambda e:
                       e.dma_start(out=ab[:], in_=ins["acs"][:, c * 64:(c + 1) * 64]))(),
                r=[("dram", "acs")], w=[(T, "acsb", c % 2)], tag=f"{T}lac{c}")
        sch.dma("sp", (lambda c=c: lambda e:
                       e.dma_start(out=xsb[:], in_=ins["xs"][c]))(),
                r=[("dram", "xs", c), (T, "xsfree")], w=[(T, "xsb")], tag=f"{T}lxs{c}")
        bclB = bcbB[c % 2]
        bclC = bcbC[c % 2]
        sch.dma("sp", (lambda c=c, bclB=bclB: lambda e:
                       e.dma_start(out=bclB[:], in_=ins["bc"][0:64, c * CHK:(c + 1) * CHK]))(),
                r=[("dram", "bc", c)], w=[(T, "bcbB", c % 2)], tag=f"{T}lbcB{c}")
        sch.dma("sp", (lambda c=c, bclC=bclC: lambda e:
                       e.dma_start(out=bclC[:], in_=ins["bc"][64:128, c * CHK:(c + 1) * CHK]))(),
                r=[("dram", "bc", c)], w=[(T, "bcbC", c % 2)], tag=f"{T}lbcC{c}")
        sch.dma("sp", (lambda c=c: lambda e:
                       e.dma_start(out=prefb[:], in_=ins["pref"][c]))(),
                r=[("dram", "pref", c), (T, "preffree")], w=[(T, "prefb")], tag=f"{T}lpr{c}")

        sch.op("act", (lambda: lambda e: e.activation(edrow[:], alpre[:], AF.Exp))(),
               r=[(T, "alpre"), (T, "edrow")], w=[(T, "edrow")], tag=f"{T}dpe{c}")
        sch.op("pe", (lambda: lambda e: e.matmul(ps[5][0:64, 0:32], cx.onesrow[0:1, 0:64],
                                                 edrow[:], start=True, stop=True))(),
               r=[(T, "edrow"), ("ones",)], w=[("psum", 5)], tag=f"{T}dpm{c}")
        sch.op("dve", (lambda: lambda e: e.tensor_copy(dpcol[:], ps[5][0:64, 0:32]))(),
               r=[("psum", 5)], w=[(T, "dpcol")], tag=f"{T}dpc{c}")
        for h in range(NH):
            sch.op("dve", (lambda h=h: lambda e:
                           e.scalar_tensor_tensor(prefb[:, h * 128:(h + 1) * 128],
                                                  Sin[:, h * 128:(h + 1) * 128],
                                                  dpcol[:, h:h + 1],
                                                  prefb[:, h * 128:(h + 1) * 128],
                                                  OP.mult, OP.add))(),
                   r=[(T, "Sin"), (T, "dpcol"), (T, "prefb")], w=[(T, "prefb")],
                   tag=f"{T}sg{c}_{h}")
        sch.op("dve", (lambda: lambda e: e.tensor_copy(sgb[:], prefb[:]))(),
               r=[(T, "prefb")], w=[(T, "sgb"), (T, "preffree")], tag=f"{T}sgc{c}")
        sch.op("dve", (lambda c=c: lambda e:
                       e.tensor_tensor(alpre[:], alpre[:], alrow[0:1, c * 32:(c + 1) * 32],
                                       OP.add))(),
               r=[(T, "alast"), (T, "edrow")], w=[(T, "alpre")], tag=f"{T}alp{c}")

        def mmtr(e, ab=ab):
            i = e.transpose(ps[5][0:32, 128:256], ab[:, 0:32], identf[:])
            return e.transpose(ps[5][0:32, 256:384], ab[:, 32:64], identf[:])
        sch.op("pe", mmtr, r=[(T, "acsb", c % 2), (T, "id")], w=[("psum", 5)],
               tag=f"{T}acmt{c}")
        sch.op("dve", (lambda: lambda e: e.tensor_copy(acm[:], ps[5][0:32, 128:384]))(),
               r=[("psum", 5)], w=[(T, "acm")], tag=f"{T}acmc{c}")
        sch.op("act", (lambda ab=ab: lambda e: e.activation(ebuf[:], ab[:], AF.Exp))(),
               r=[(T, "acsb", c % 2)], w=[(T, "ebuf")], tag=f"{T}ebe{c}")

        def mmbtc(e, bclB=bclB, bclC=bclC):
            e.matmul(ps[4][0:128, 0:256], bclB[:, 0:128], bclC[:, 0:256],
                     start=True, stop=True)
            return e.matmul(ps[4][0:128, 256:384], bclB[:, 128:256],
                            bclC[:, 128:256], start=True, stop=True)
        sch.op("pe", mmbtc, r=[(T, "bcbB", c % 2), (T, "bcbC", c % 2)], w=[("psum", 4)], tag=f"{T}btc{c}")
        sch.op("dve", (lambda: lambda e:
                       e.tensor_tensor(btc0[:], ps[4][0:128, 0:256], maskA[:], OP.mult))(),
               r=[("psum", 4), (T, "maskA")], w=[(T, "btc0")], tag=f"{T}btm0{c}")
        sch.op("dve", (lambda: lambda e:
                       e.tensor_tensor(btc1[:], ps[4][0:128, 256:384], maskA[:, 0:128],
                                       OP.mult))(),
               r=[("psum", 4), (T, "maskA")], w=[(T, "btc1")], tag=f"{T}btm1{c}")

        for hb in range(NH // HB):
            for hh in range(HB):
                h = hb * HB + hh
                def mmrb(e, h=h):
                    e.matmul(ps[5][0:128, 128:384], sel[:, h * 128:(h + 1) * 128],
                             acm[:, 0:256], start=True, stop=True)
                    return e.matmul(ps[7][0:128, 0:128], sel[:, h * 128:(h + 1) * 128],
                                    acm[:, 128:256], start=True, stop=True)
                sch.op("pe", mmrb, r=[(T, "acm"), (T, "sel")],
                       w=[("psum", 5), ("psum", 7)], tag=f"{T}rb{c}_{h}")
                o0 = hh * 384
                sch.op("dve", (lambda h=h, o0=o0, ab=ab: lambda e:
                               e.tensor_scalar(t1b[:, o0:o0 + 256], ps[5][0:128, 128:384],
                                               ab[:, h:h + 1], 0.0, OP.subtract, OP.min))(),
                       r=[("psum", 5), (T, "acsb", c % 2)], w=[(T, "t1", hh)],
                       tag=f"{T}t10_{c}_{h}")
                sch.op("dve", (lambda h=h, o0=o0, ab=ab: lambda e:
                               e.tensor_scalar(t1b[:, o0 + 256:o0 + 384], ps[7][0:128, 0:128],
                                               ab[:, 32 + h:32 + h + 1], 0.0,
                                               OP.subtract, OP.min))(),
                       r=[("psum", 7), (T, "acsb", c % 2)], w=[(T, "t1b", hh)],
                       tag=f"{T}t11_{c}_{h}")
            sch.op("act", (lambda: lambda e: e.activation(exb[:], t1b[:], AF.Exp))(),
                   r=[(T, "t1", hh2) for hh2 in range(HB)] +
                     [(T, "t1b", hh2) for hh2 in range(HB)],
                   w=[(T, "exb")], tag=f"{T}exp{c}_{hb}")
            for hh in range(HB):
                h = hb * HB + hh
                o0 = hh * 384
                sch.op("dve", (lambda o0=o0: lambda e:
                               e.tensor_tensor(gtb[:, o0:o0 + 256], exb[:, o0:o0 + 256],
                                               btc0[:], OP.mult))(),
                       r=[(T, "exb"), (T, "btc0")], w=[(T, "gt", hh, 0)],
                       tag=f"{T}gt0_{c}_{h}")
                sch.op("dve", (lambda o0=o0: lambda e:
                               e.tensor_tensor(gtb[:, o0 + 256:o0 + 384],
                                               exb[:, o0 + 256:o0 + 384], btc1[:], OP.mult))(),
                       r=[(T, "exb"), (T, "btc1")], w=[(T, "gt", hh, 1)],
                       tag=f"{T}gt1_{c}_{h}")
                bA = 2 * (hh % 2)
                bB = bA + 1

                def mmy(e, h=h, o0=o0, bA=bA, bB=bB, c=c):
                    e.matmul(ps[bB][0:128, 0:128], bcbC[c % 2][:, 0:128],
                             sgb[:, h * 128:(h + 1) * 128], start=True, stop=True)
                    e.matmul(ps[bB][0:128, 128:256], bcbC[c % 2][:, 128:256],
                             sgb[:, h * 128:(h + 1) * 128], start=True, stop=True)
                    e.matmul(ps[bA][0:128, 0:128], gtb[:, o0:o0 + 128],
                             xsb[:, (2 * h) * 128:(2 * h) * 128 + 128],
                             start=True, stop=True)
                    e.matmul(ps[bA][0:128, 128:256], gtb[:, o0 + 128:o0 + 256],
                             xsb[:, (2 * h) * 128:(2 * h) * 128 + 128],
                             start=True, stop=False)
                    return e.matmul(ps[bA][0:128, 128:256], gtb[:, o0 + 256:o0 + 384],
                                    xsb[:, (2 * h + 1) * 128:(2 * h + 1) * 128 + 128],
                                    start=False, stop=True)
                sch.op("pe", mmy, r=[(T, "gt", hh, 0), (T, "gt", hh, 1), (T, "xsb"),
                                     (T, "sgb"), (T, "bcbC", c % 2)],
                       w=[("psum", bA), ("psum", bB)], tag=f"{T}y{c}_{h}")
                for lam in range(2):
                    yb = ybig[lam]
                    sch.op("act", (lambda h=h, lam=lam, bB=bB, yb=yb: lambda e:
                                   e.activation(yb[:, h * 128:(h + 1) * 128],
                                                ps[bB][0:128, lam * 128:lam * 128 + 128],
                                                AF.Copy,
                                                scale=ebuf[:, lam * 32 + h:lam * 32 + h + 1]))(),
                           r=[("psum", bB), (T, "ebuf")], w=[(T, "ybo", lam, h)],
                           tag=f"{T}ye{c}_{h}_{lam}")
                    sch.op("dve", (lambda h=h, lam=lam, bA=bA, yb=yb: lambda e:
                                   e.tensor_tensor(yb[:, h * 128:(h + 1) * 128],
                                                   yb[:, h * 128:(h + 1) * 128],
                                                   ps[bA][0:128, lam * 128:lam * 128 + 128],
                                                   OP.add))(),
                           r=[("psum", bA), (T, "ybo", lam, h)], w=[(T, "yb", lam, h)],
                           tag=f"{T}ya{c}_{h}_{lam}")
        sch.op("dve", lambda e: e.engine_nop(),
               r=[(T, "yb", lam, h2) for lam in range(2) for h2 in range(NH)],
               w=[(T, "xsfree")], tag=f"{T}xsf{c}")

        for lam in range(2):
            t = lam2[lam]
            yb = ybig[lam]
            sch.dma("sp", (lambda t=t: lambda e:
                           e.dma_start(out=ztl[:], in_=ins["z"][t * 128:(t + 1) * 128, :]))(),
                    r=[("dram", "z", t // 4, t % 4, zb2) for zb2 in range(8)] +
                      [(T, "ztlfree")],
                    w=[(T, "ztl")], tag=f"{T}lz{t}")
            sch.op("act", (lambda: lambda e: e.activation(szb[:], ztl[:], AF.Silu))(),
                   r=[(T, "ztl")], w=[(T, "szb"), (T, "ztlfree")], tag=f"{T}sz{t}")
            sch.op("dve", (lambda yb=yb: lambda e:
                           e.tensor_tensor(yb[:], yb[:], szb[:], OP.mult))(),
                   r=[(T, "yb", lam, h2) for h2 in range(NH)] + [(T, "szb")],
                   w=[(T, "g", lam)], tag=f"{T}gm{t}")
            sch.op("dve", (lambda yb=yb, t=t: lambda e:
                            e.reduce_sum(cx.sumcol[:, t % 64:t % 64 + 1], yb[:], axis=AX.X))(),
                   r=[(T, "g", lam)], w=[(T, "gsum", t)], tag=f"{T}gs{t}")
            sch.op("act", (lambda yb=yb, t=t: lambda e:
                           e.activation(szb[:], yb[:], AF.Square,
                                        accum_out=cx.sumsqcol[:, t % 64:t % 64 + 1]))(),
                   r=[(T, "g", lam), (T, "gsum", t)], w=[(T, "gssq", t), (T, "szb")],
                   tag=f"{T}gq{t}")
            sch.op("dve", (lambda t=t: lambda e:
                           e.tensor_scalar(mu2[:, t:t + 1], cx.sumcol[:, t % 64:t % 64 + 1],
                                           1.0 / DI, None, OP.mult))(),
                   r=[(T, "gsum", t)], w=[(T, "gmu", t)], tag=f"{T}gmu{t}")
            sch.op("dve", (lambda t=t: lambda e:
                           e.tensor_tensor(var2[:, t:t + 1], mu2[:, t:t + 1],
                                           mu2[:, t:t + 1], OP.mult))(),
                   r=[(T, "gmu", t)], w=[(T, "gm2", t)], tag=f"{T}gm2{t}")
            sch.op("dve", (lambda t=t: lambda e:
                           e.scalar_tensor_tensor(var2[:, t:t + 1],
                                                  cx.sumsqcol[:, t % 64:t % 64 + 1], 1.0 / DI,
                                                  var2[:, t:t + 1], OP.mult, OP.subtract))(),
                   r=[(T, "gssq", t), (T, "gm2", t)], w=[(T, "gvp", t)], tag=f"{T}gv{t}")
            sch.op("dve", (lambda t=t: lambda e:
                           e.tensor_scalar(var2[:, t:t + 1], var2[:, t:t + 1], EPS,
                                           None, OP.add))(),
                   r=[(T, "gvp", t), (T, "v2i")], w=[(T, "gve", t)], tag=f"{T}gvee{t}")
            sch.op("act", (lambda t=t: lambda e:
                           e.activation(rstd2[:, t:t + 1], var2[:, t:t + 1], AF.Sqrt))(),
                   r=[(T, "gve", t)], w=[(T, "gsd", t)], tag=f"{T}gsq{t}")
            sch.op("dve", (lambda t=t: lambda e:
                           e.reciprocal(rstd2[:, t:t + 1], rstd2[:, t:t + 1]))(),
                   r=[(T, "gsd", t)], w=[(T, "grs", t)], tag=f"{T}gri{t}")
            sch.op("dve", (lambda yb=yb, t=t: lambda e:
                           e.tensor_scalar(yb[:], yb[:], mu2[:, t:t + 1],
                                           rstd2[:, t:t + 1], OP.subtract, OP.mult))(),
                   r=[(T, "g", lam), (T, "gmu", t), (T, "grs", t)], w=[(T, "gn", lam)],
                   tag=f"{T}gn{t}")
            yft = yf[lam]
            sch.dma("act", (lambda yb=yb, yft=yft: lambda e:
                            e.dma_start_transpose(
                                out=yft[:].rearrange("p (k t) -> p k t", k=32),
                                in_=yb[:].rearrange("t (k p) -> t k p", k=32)))(),
                    r=[(T, "gn", lam)], w=[(T, "yf", lam)], tag=f"{T}ytr{t}")
            x2b = x2t[lam]
            sch.dma("act", (lambda x2b=x2b, t=t: lambda e:
                            e.dma_start(out=x2b[:],
                                        in_=x1_dram[t * 128:(t + 1) * 128, :]))(),
                    r=dram_tile_keys(x1_name, t * 128, 128) + [(T, "x2free", lam)],
                    w=[(T, "x2b", lam)], tag=f"{T}lx1{t}")

        for fb in range(4):
            f0 = fb * 512
            for kc in range(32):
                wb2 = wsb[(kc // 4) % 2]
                if kc % 4 == 0:
                    sch.dma("sp", (lambda kc=kc, f0=f0, wb2=wb2: lambda e:
                                   e.dma_start(out=wb2[:].rearrange("p (q f) -> p q f", q=4),
                                               in_=W["wopsb"][kc:kc + 4, :, f0:f0 + 512]
                                               .rearrange("q p f -> p q f")))(),
                            r=[("dram", "wopsb", kc + qq) for qq in range(4)],
                            w=[(T, "wsb", (kc // 4) % 2)],
                            tag=f"{T}lws{c}_{fb}_{kc}")

                def mmo(e, kc=kc, f0=f0, wb2=wb2, fb=fb):
                    i = None
                    woff = (kc % 4) * 512
                    for lam in range(2):
                        bank = fb % 2 * 2 + lam
                        if kc == 0:
                            i = e.matmul(ps[bank][0:128, 0:512], cx.onesrow[0:1, 0:128],
                                         oBr[0:1, f0:f0 + 512], start=True, stop=False)
                        i = e.matmul(ps[bank][0:128, 0:512],
                                     yf[lam][:, kc * 128:(kc + 1) * 128],
                                     wb2[:, woff:woff + 512],
                                     start=False, stop=(kc == 31))
                    return i
                wkeys = ([("psum", fb % 2 * 2), ("psum", fb % 2 * 2 + 1)]
                         if kc in (0, 31) else [])
                sch.op("pe", mmo,
                       r=[(T, "wsb", (kc // 4) % 2), (T, "yf", 0), (T, "yf", 1), ("ones",)] + oB_keys,
                       w=wkeys, tag=f"{T}mo{c}_{fb}_{kc}")
            for lam in range(2):
                t = lam2[lam]
                x2b = x2t[lam]
                bank = fb % 2 * 2 + lam
                sch.op("dve", (lambda x2b=x2b, bank=bank, f0=f0: lambda e:
                               e.tensor_tensor(x2b[:, f0:f0 + 512], ps[bank][0:128, 0:512],
                                               x2b[:, f0:f0 + 512], OP.add))(),
                       r=[("psum", bank), (T, "x2b", lam)], w=[(T, "x2d", lam, fb)],
                       tag=f"{T}xa{t}_{fb}")
        for lam in range(2):
            t = lam2[lam]
            x2b = x2t[lam]
            sch.dma("act", (lambda x2b=x2b, t=t: lambda e:
                            e.dma_start(out=x2_dram[t * 128:(t + 1) * 128, :], in_=x2b[:]))(),
                    r=[(T, "x2d", lam, fb2) for fb2 in range(4)],
                    w=dram_tile_keys(x2_name, t * 128, 128) + [(T, "x2free", lam)],
                    tag=f"{T}sx2{t}")


def emit_weight_prep(cx, pairs):
    sch = cx.sch
    for nm, src, dst, nparts in pairs:
        for i in range(nparts):
            sch.dma("pool", (lambda src=src, dst=dst, i=i: lambda e:
                             e.dma_start(out=dst[i], in_=src[i]))(),
                    w=[("dram", nm, i)], tag=f"prep_{nm}_{i}")


def declare_l1(nc, ntok):
    nchunk = ntok // CHK
    EI, EO = "ExternalInput", "ExternalOutput"
    t = {}
    t["u"] = nc.dram_tensor("u", [ntok + 3, D], F32, kind=EI).ap()
    t["w0s"] = nc.dram_tensor("w0s", [64, 128, 16, 128], F32, kind=EI).ap()
    t["w1s"] = nc.dram_tensor("w1s", [64, 128, 16, 128], F32, kind=EI).ap()
    t["b0"] = nc.dram_tensor("b0", [1, DH], F32, kind=EI).ap()
    t["b1"] = nc.dram_tensor("b1", [1, D], F32, kind=EI).ap()
    for nm, shp in [("wipx", [33, 128, 16, 128]), ("wipz", [8, 4, 128, 2048]),
                    ("wipdt", [128, 16, 32]), ("cwp", [128, 33, 4]), ("cbp", [128, 33]),
                    ("xbcbc", [128, 33]), ("zbias", [8, 512]), ("dtb", [1, 32]),
                    ("dtbias", [1, 32]), ("alog", [1, 32]), ("tri", [128, 128]),
                    ("onesq", [128, 128]), ("hmask", [128, 1])]:
        t[nm] = nc.dram_tensor(nm, shp, F32, kind=EI).ap()
    t["x1"] = nc.dram_tensor("x1", [ntok + 3, D], F32, kind=EO).ap()
    t["o_z"] = nc.dram_tensor("o_z", [ntok, DI], BF16, kind=EO).ap()
    t["o_xs"] = nc.dram_tensor("o_xs", [nchunk, 128, 8192], BF16, kind=EO).ap()
    t["o_bc"] = nc.dram_tensor("o_bc", [128, ntok], BF16, kind=EO).ap()
    t["o_pref"] = nc.dram_tensor("o_pref", [nchunk, 64, DI], F32, kind=EO).ap()
    t["o_acs"] = nc.dram_tensor("o_acs", [128, nchunk * 64], F32, kind=EO).ap()
    t["o_alast"] = nc.dram_tensor("o_alast", [1, nchunk * 32], F32, kind=EO).ap()
    t["o_sfin"] = nc.dram_tensor("o_sfin", [64, DI], F32, kind=EO).ap()
    t["o_rdec"] = nc.dram_tensor("o_rdec", [1, 32], F32, kind=EO).ap()
    return t


def build_l1(ntok):
    nc = bass.Bass("TRN2", target_bir_lowering=False)
    t = declare_l1(nc, ntok)
    nchunk = ntok // CHK
    t["w0bs"] = nc.dram_tensor("w0bs", [64, 128, 16, 128], BF16, kind="Internal").ap()
    t["w1bs"] = nc.dram_tensor("w1bs", [64, 128, 16, 128], BF16, kind="Internal").ap()
    t["wipxs"] = nc.dram_tensor("wipxs", [33, 128, 16, 128], BF16, kind="Internal").ap()
    t["wipzs"] = nc.dram_tensor("wipzs", [8, 4, 128, 2048], BF16, kind="Internal").ap()
    cx = Ctx(nc, ntok)
    sch = cx.sch
    emit_weight_prep(cx, [("w0bs", t["w0s"], t["w0bs"], 64),
                          ("w1bs", t["w1s"], t["w1bs"], 64),
                          ("wipxs", t["wipx"], t["wipxs"], 33)])
    with cx.phase():
        build_ffn_phase(cx, t["w0bs"], t["w1bs"], t["b0"], t["b1"],
                        t["u"], "u", t["x1"], "x1", ntok + 3, "p1")
    for zb in range(8):
        for kcq in range(4):
            sch.dma("pool", (lambda zb=zb, kcq=kcq: lambda e:
                             e.dma_start(out=t["wipzs"][zb, kcq], in_=t["wipz"][zb, kcq]))(),
                    w=[("dram", "wipzs", zb, kcq)], tag=f"prep_wipz_{zb}_{kcq}")
    W = {k: t[k] for k in ["wipx", "wipz", "wipdt", "cwp", "cbp", "xbcbc",
                           "dtb", "dtbias", "alog", "tri", "onesq", "hmask"]}
    W["zb"] = t["zbias"]
    W["wipx"] = t["wipxs"]
    W["wipz"] = t["wipzs"]
    outs = dict(z=t["o_z"], xs=t["o_xs"], bc=t["o_bc"], pref=t["o_pref"],
                acs=t["o_acs"], alast=t["o_alast"], sfin=t["o_sfin"], rdec=t["o_rdec"])
    with cx.phase():
        build_p2a(cx, W, t["x1"], "x1", outs, ntok)
    fin = (dram_tile_keys("x1", 0, ntok + 3) +
           [("dram", "sfin"), ("dram", "acs"), ("dram", "alast"), ("dram", "rdec")] +
           [("dram", "pref", c) for c in range(nchunk)] +
           [("dram", "xs", c) for c in range(nchunk)] +
           [("dram", "bc", c) for c in range(nchunk)] +
           [("dram", "z", g, tt, zb) for g in range(ntok // GRP) for tt in range(4)
            for zb in range(8)])
    sch.op("sp", None, r=fin, tag="fin")
    sch.emit(nc)
    return nc


def declare_l2(nc, ntok, dbg_x2=False):
    nchunk = ntok // CHK
    EI, EO = "ExternalInput", "ExternalOutput"
    t = {}
    t["x1"] = nc.dram_tensor("x1", [ntok + 3, D], F32, kind=EI).ap()
    t["i_z"] = nc.dram_tensor("i_z", [ntok, DI], BF16, kind=EI).ap()
    t["i_xs"] = nc.dram_tensor("i_xs", [nchunk, 128, 8192], BF16, kind=EI).ap()
    t["i_bc"] = nc.dram_tensor("i_bc", [128, ntok], BF16, kind=EI).ap()
    t["i_pref"] = nc.dram_tensor("i_pref", [nchunk, 64, DI], F32, kind=EI).ap()
    t["i_acs"] = nc.dram_tensor("i_acs", [128, nchunk * 64], F32, kind=EI).ap()
    t["i_alast"] = nc.dram_tensor("i_alast", [1, nchunk * 32], F32, kind=EI).ap()
    t["i_pstate"] = nc.dram_tensor("i_pstate", [3, 64, DI], F32, kind=EI).ap()
    t["i_pdec"] = nc.dram_tensor("i_pdec", [3, 32], F32, kind=EI).ap()
    t["wops"] = nc.dram_tensor("wops", [32, 128, D], F32, kind=EI).ap()
    t["wopsb"] = nc.dram_tensor("wopsb", [32, 128, D], BF16, kind="Internal").ap()
    t["lnwc"] = nc.dram_tensor("lnwc", [128, 32], F32, kind=EI).ap()
    t["lnbc"] = nc.dram_tensor("lnbc", [128, 32], F32, kind=EI).ap()
    t["opb"] = nc.dram_tensor("opb", [1, D], F32, kind=EI).ap()
    t["sel"] = nc.dram_tensor("sel", [32, DI], F32, kind=EI).ap()
    t["maskA"] = nc.dram_tensor("maskA", [128, 256], F32, kind=EI).ap()
    t["identf"] = nc.dram_tensor("identf", [128, 128], F32, kind=EI).ap()
    t["w0s"] = nc.dram_tensor("w0s", [64, 128, 16, 128], F32, kind=EI).ap()
    t["w1s"] = nc.dram_tensor("w1s", [64, 128, 16, 128], F32, kind=EI).ap()
    t["b0"] = nc.dram_tensor("b0", [1, DH], F32, kind=EI).ap()
    t["b1"] = nc.dram_tensor("b1", [1, D], F32, kind=EI).ap()
    t["x2"] = nc.dram_tensor("x2", [ntok, D], F32, kind=EO if dbg_x2 else "Internal").ap()
    t["o_out"] = nc.dram_tensor("o_out", [ntok, D], F32, kind=EO).ap()
    return t


def build_l2(ntok, dbg_x2=False):
    nc = bass.Bass("TRN2", target_bir_lowering=False)
    t = declare_l2(nc, ntok, dbg_x2)
    t["w0bs"] = nc.dram_tensor("w0bs", [64, 128, 16, 128], BF16, kind="Internal").ap()
    t["w1bs"] = nc.dram_tensor("w1bs", [64, 128, 16, 128], BF16, kind="Internal").ap()
    cx = Ctx(nc, ntok)
    sch = cx.sch
    emit_weight_prep(cx, [("w0bs", t["w0s"], t["w0bs"], 64),
                          ("w1bs", t["w1s"], t["w1bs"], 64)])
    W = {k: t[k] for k in ["wops", "wopsb", "lnwc", "lnbc", "opb", "sel",
                           "maskA", "identf"]}
    ins = dict(z=t["i_z"], xs=t["i_xs"], bc=t["i_bc"], pref=t["i_pref"],
               acs=t["i_acs"], alast=t["i_alast"], pstate=t["i_pstate"],
               pdec=t["i_pdec"])
    with cx.phase():
        build_p2b(cx, W, ins, t["x1"], "x1", t["x2"], "x2", ntok)
    with cx.phase():
        build_ffn_phase(cx, t["w0bs"], t["w1bs"], t["b0"], t["b1"],
                        t["x2"], "x2", t["o_out"], "o_out", ntok, "p3")
    sch.op("sp", None, r=dram_tile_keys("o_out", 0, ntok), tag="fin")
    sch.emit(nc)
    return nc


def stage_weights_l1(inp):
    w = {}
    w["w0s"] = prep_w0(inp["L0_ffn_W0"])
    w["w1s"] = prep_w1(inp["L0_ffn_W1"])
    w["b0"] = inp["L0_ffn_bias0"].reshape(1, -1).copy()
    w["b1"] = inp["L0_ffn_bias1"].reshape(1, -1).copy()
    Wip = inp["L0_attn_in_proj_w"]
    ipb = inp["L0_attn_in_proj_b"]
    w["wipx"] = np.ascontiguousarray(
        Wip[4096:8320].reshape(33, 128, 16, 128).transpose(0, 3, 2, 1))
    w["wipz"] = np.ascontiguousarray(
        Wip[0:4096].reshape(8, 512, 4, 4, 128).transpose(0, 2, 4, 3, 1)
        .reshape(8, 4, 128, 2048))
    w["wipdt"] = np.ascontiguousarray(
        Wip[8320:8352].reshape(32, 16, 128).transpose(2, 1, 0))
    cw = inp["L0_attn_conv_w"]
    w["cwp"] = np.ascontiguousarray(cw[:4224, 0, :].reshape(33, 128, 4).transpose(1, 0, 2))
    w["cbp"] = np.ascontiguousarray(inp["L0_attn_conv_b"][:4224].reshape(33, 128).T)
    w["xbcbc"] = np.ascontiguousarray(ipb[4096:8320].reshape(33, 128).T)
    w["zbias"] = ipb[0:4096].reshape(8, 512).copy()
    w["dtb"] = ipb[8320:8352].reshape(1, -1).copy()
    w["dtbias"] = inp["L0_attn_dt_bias"].reshape(1, -1).copy()
    w["alog"] = inp["L0_attn_A_log"].reshape(1, -1).copy()
    w["tri"] = np.ascontiguousarray(np.tril(np.ones((128, 128), np.float32)).T)
    w["onesq"] = np.ones((128, 128), np.float32)
    return w


def stage_weights_l2(inp):
    w = {}
    w["wops"] = np.ascontiguousarray(
        inp["L0_attn_out_proj_w"].reshape(D, 32, 128).transpose(1, 2, 0))
    w["lnwc"] = np.ascontiguousarray(inp["L0_attn_ln_w"].reshape(32, 128).T)
    w["lnbc"] = np.ascontiguousarray(inp["L0_attn_ln_b"].reshape(32, 128).T)
    w["opb"] = inp["L0_attn_out_proj_b"].reshape(1, -1).copy()
    w["sel"] = np.kron(np.eye(32, dtype=np.float32), np.ones((1, 128), np.float32))
    w["maskA"] = (np.arange(256)[None, :] >= np.arange(128)[:, None]).astype(np.float32)
    w["identf"] = np.eye(128, dtype=np.float32)
    w["w0s"] = prep_w0(inp["L0_ffn_W0"])
    w["w1s"] = prep_w1(inp["L0_ffn_W1"])
    w["b0"] = inp["L0_ffn_bias0"].reshape(1, -1).copy()
    w["b1"] = inp["L0_ffn_bias1"].reshape(1, -1).copy()
    return w


NTOK = 4096


def kernel(**inputs):
    u = np.ascontiguousarray(inputs["L0_attn_u"], dtype=np.float32)
    B, L, _ = u.shape
    npb = L // NTOK
    w1 = stage_weights_l1(inputs)
    w2 = stage_weights_l2(inputs)
    nc1 = build_l1(NTOK)
    maps1 = []
    for k in range(NCORES):
        b, pos = k // npb, k % npb
        s = pos * NTOK
        us = np.zeros((NTOK + 3, D), np.float32)
        us[:NTOK] = u[b, s:s + NTOK]
        if pos > 0:
            us[NTOK:] = u[b, s - 3:s]
        m = dict(w1)
        m["u"] = us
        m["hmask"] = np.full((128, 1), 0.0 if pos == 0 else 1.0, np.float32)
        maps1.append(m)
    r1 = run_bass_kernel_spmd(nc1, maps1, core_ids=list(range(NCORES)))

    nc2 = build_l2(NTOK)
    maps2 = []
    for k in range(NCORES):
        b, pos = k // npb, k % npb
        o1 = r1.results[k]
        pstate = np.zeros((3, 64, DI), np.float32)
        pdec = np.zeros((3, 32), np.float32)
        for j in range(min(3, pos)):
            pk = k - 1 - j
            pstate[2 - j] = r1.results[pk]["o_sfin"]
            pdec[2 - j] = r1.results[pk]["o_rdec"].reshape(32)
        m = dict(w2)
        m["x1"] = o1["x1"]
        m["i_z"] = o1["o_z"]
        m["i_xs"] = o1["o_xs"]
        m["i_bc"] = o1["o_bc"]
        m["i_pref"] = o1["o_pref"]
        m["i_acs"] = o1["o_acs"]
        m["i_alast"] = o1["o_alast"]
        m["i_pstate"] = pstate
        m["i_pdec"] = pdec
        maps2.append(m)
    r2 = run_bass_kernel_spmd(nc2, maps2, core_ids=list(range(NCORES)))

    out = np.empty((B, L, D), np.float32)
    for k in range(NCORES):
        b, pos = k // npb, k % npb
        out[b, pos * NTOK:(pos + 1) * NTOK] = r2.results[k]["o_out"]
    return out

